# version 01
# baseline (speedup 1.0000x reference)
import numpy as np
from contextlib import ExitStack
import concourse.bass as bass
import concourse.mybir as mybir
from concourse.alu_op_type import AluOpType as ALU

F32 = mybir.dt.float32
BF16 = mybir.dt.bfloat16
I32 = mybir.dt.int32
AF = mybir.ActivationFunctionType
AX = mybir.AxisListType

H, P, N, DM, DI, DPROJ = 32, 64, 64, 1024, 2048, 5312
OFF_Z, OFF_XIN, OFF_B, OFF_C = 0, 2048, 4096, 4160
OFF_DT, OFF_THETA, OFF_LAM = 4224, 4256, 5280
TPC, NW, SEG, Q = 2048, 4, 512, 128
NTB = 16
IFF = 3072
BIG = 3.0e38
EPS = 1e-5
INV2PI = float(1.0 / (2.0 * np.pi))
TWOPI = float(2.0 * np.pi)
HALFPI = float(np.pi / 2.0)
RG_SEQ = [[0, 1, 2, 3], [4, 5, 6, 7]]
RG_ALL = [[0, 1, 2, 3, 4, 5, 6, 7]]


class K:
    ENGINES = ("sync", "tensor", "vector", "scalar", "gpsimd")

    def __init__(self, stage=99, dbg_names=()):
        self.stage = stage
        self.dbg_req = set(dbg_names)
        self.dbg_out = []
        self.nc = bass.Bass("TRN2", target_bir_lowering=False,
                            detect_race_conditions=False)
        self.es = ExitStack()
        self.D = {}
        self.T = {}
        self.ops = {e: [] for e in self.ENGINES}
        self.semh = {}
        self.semn = {}
        self.wm = {e: {} for e in self.ENGINES}
        self.recent = {e: [] for e in self.ENGINES}
        self.tw = {}
        self.trd = {}
        self._dbg_ctr = 0

    def _sem(self, name):
        if name not in self.semh:
            self.semh[name] = self.es.enter_context(self.nc.semaphore(name))
            self.semn[name] = 0
        return self.semh[name]

    @staticmethod
    def _apnames(aps):
        out = set()
        for a in aps:
            if a is None:
                continue
            try:
                out.add(a.tensor.name)
            except Exception:
                try:
                    out.add(a.name)
                except Exception:
                    pass
        return out

    def ex(self, eng, emitfn, waits=(), pub=None, pubk=None, rd=(), wr=(),
           ckp=True):
        if ckp and eng != "sync":
            for a in list(rd) + list(wr):
                if a is None or isinstance(a, (int, float)):
                    continue
                try:
                    sp = a.offset // a.ap[0][0]
                except Exception:
                    continue
                if sp % 32 != 0:
                    import traceback
                    st_ = traceback.extract_stack()
                    print(f"PARTSTART-BAD sp={sp} shape={a.shape} at " +
                          " <- ".join(f"{f.filename.split('/')[-1]}:{f.lineno}"
                                      for f in st_[-5:-1]))
        wm = self.wm[eng]
        waits = [t for t in waits if t is not None]
        rdn = self._apnames(rd)
        wrn = self._apnames(wr)
        for n in rdn:
            t = self.tw.get(n)
            if t is not None:
                waits.append(t)
        for n in wrn:
            t = self.tw.get(n)
            if t is not None:
                waits.append(t)
            for s, v in self.trd.get(n, {}).items():
                waits.append((s, v))
        if eng in ("vector", "scalar", "gpsimd"):
            names = rdn | wrn
            for wn, tok in self.recent[eng]:
                if wn & names:
                    waits.append(tok)
        needed = {}
        for t in waits:
            if t is None:
                continue
            s, v = t
            if v > wm.get(s, 0) and v > needed.get(s, 0):
                needed[s] = v
        for s, v in needed.items():
            h = self._sem(s)
            self.ops[eng].append(lambda e, h=h, v=v: e.wait_ge(h, v))
            wm[s] = v
        if pub is None:
            pub = {"sync": "d_s", "scalar": "sE", "vector": "vE",
                   "tensor": "tE", "gpsimd": "gE"}[eng]
        if pubk is None:
            pubk = 16 if pub.startswith("d_") else 1
        h = self._sem(pub)
        self.semn[pub] += pubk
        val = self.semn[pub]
        def f(e, emitfn=emitfn, h=h, k=pubk):
            emitfn(e).then_inc(h, k)
        self.ops[eng].append(f)
        tok = (pub, val)
        if eng in ("vector", "scalar", "gpsimd") and wrn:
            self.recent[eng].append((wrn, tok))
            if len(self.recent[eng]) > 12:
                self.recent[eng].pop(0)
        for n in wrn:
            self.tw[n] = tok
            self.trd[n] = {}
        for n in rdn:
            d = self.trd.setdefault(n, {})
            if tok[1] > d.get(tok[0], 0):
                d[tok[0]] = tok[1]
        return tok

    def barrier(self):
        toks = {}
        for e in self.ENGINES:
            s = "bar_" + e
            h = self._sem(s)
            self.semn[s] += 1
            v = self.semn[s]
            if e == "sync":
                self.ops[e].append(lambda eng, h=h: eng.nop().then_inc(h, 1))
            elif e in ("tensor", "scalar"):
                self.ops[e].append(lambda eng, h=h: eng.drain().then_inc(h, 1))
            else:
                self.ops[e].append(
                    lambda eng, h=h: eng.engine_nop().then_inc(h, 1))
            toks[e] = (s, v)
        dma_now = [(s, self.semn[s]) for s in self.semn
                   if s.startswith("d_") and self.semn[s] > 0]
        for e in self.ENGINES:
            need = [toks[e2] for e2 in self.ENGINES if e2 != e] + dma_now
            for s, v in need:
                if v > self.wm[e].get(s, 0):
                    h = self._sem(s)
                    self.ops[e].append(lambda eng, h=h, v=v: eng.wait_ge(h, v))
                    self.wm[e][s] = v

    def mm(self, out, lhsT, rhs, start=True, stop=True, w=(), tp=None):
        return self.ex("tensor", lambda e: e.matmul(
            out, lhsT, rhs, start=start, stop=stop, tile_position=tp), waits=w,
            rd=[lhsT, rhs], wr=[out])

    def tr(self, out, in_, idn=None, w=()):
        if idn is None:
            kp = in_.shape[0]
            b0 = in_.base_partition() if callable(in_.base_partition) else in_.base_partition
            idn = self.T["consts"][b0:b0 + kp, b0:b0 + kp]
        return self.ex("tensor", lambda e: e.transpose(out, in_, idn), waits=w,
                       rd=[in_, idn], wr=[out])

    @staticmethod
    def canon_base(ap):
        try:
            b = ap.base_partition() if callable(ap.base_partition) else ap.base_partition
            return b
        except AssertionError as ex:
            import re
            m = re.search(r"got (\d+)", str(ex))
            if m:
                n = int(m.group(1))
                return 64 if n >= 64 else (32 if n >= 32 else 0)
            raise

    @staticmethod
    def start_part(ap):
        try:
            return ap.offset // ap.ap[0][0]
        except Exception:
            return None

    def _ckbase(self, tag, a, b):
        try:
            ba, bb = self.start_part(a), self.start_part(b)
            if ba is None or bb is None:
                ba, bb = self.canon_base(a), self.canon_base(b)
            if ba != bb:
                import traceback
                st_ = traceback.extract_stack()
                print(f"{tag}-BASEMISMATCH a={ba} b={bb} shapes={a.shape}/{b.shape} at " +
                      " <- ".join(f"{f.filename.split('/')[-1]}:{f.lineno}" for f in st_[-5:-2]))
        except Exception as ex:
            print(f"{tag}-BASECHECKFAIL {type(ex).__name__} {ex}")

    def tt(self, out, a, b, op, w=(), eng="vector"):
        self._ckbase("TT", a, b)
        return self.ex(eng, lambda e: e.tensor_tensor(out, a, b, op), waits=w,
                       rd=[a, b], wr=[out])

    def ts(self, out, a, s1, s2=None, op0=ALU.mult, op1=None, w=(), eng="vector"):
        rds = [a] + [x for x in (s1, s2) if not isinstance(x, (int, float, type(None)))]
        if op1 is None:
            return self.ex(eng, lambda e: e.tensor_scalar(out, a, s1, None, op0),
                           waits=w, rd=rds, wr=[out])
        return self.ex(eng, lambda e: e.tensor_scalar(out, a, s1, s2, op0, op1),
                       waits=w, rd=rds, wr=[out])

    def stt(self, out, in0, scalar, in1, op0, op1, w=()):
        self._ckbase("STT", in0, in1)
        rds = [in0, in1] + ([scalar] if not isinstance(scalar, (int, float)) else [])
        return self.ex("vector", lambda e: e.scalar_tensor_tensor(
            out, in0, scalar, in1, op0, op1), waits=w, rd=rds, wr=[out])

    def act(self, out, in_, f, bias=None, scale=None, w=()):
        kw = {}
        if bias is not None:
            kw["bias"] = bias
        if scale is not None:
            kw["scale"] = scale
        rds = [in_] + [v for kk, v in kw.items() if not isinstance(v, (int, float))]
        return self.ex("scalar", lambda e: e.activation(out, in_, f, **kw), waits=w,
                       rd=rds, wr=[out])

    def red(self, out, in_, op=ALU.add, absv=None, w=(), eng="vector"):
        return self.ex(eng, lambda e: e.tensor_reduce(
            out, in_, AX.X, op, apply_absolute_value=absv), waits=w,
            rd=[in_], wr=[out])

    def scan(self, out, d0, d1, initial, op0, op1, w=()):
        self._ckbase("SCAN", d0, d1)
        rds = [d0, d1] + ([initial] if not isinstance(initial, (int, float)) else [])
        return self.ex("vector", lambda e: e.tensor_tensor_scan(
            out, d0, d1, initial, op0, op1), waits=w, rd=rds, wr=[out])

    def cp(self, out, in_, w=(), eng="vector"):
        if eng == "scalar":
            return self.ex("scalar", lambda e: e.copy(out, in_), waits=w,
                           rd=[in_], wr=[out])
        return self.ex(eng, lambda e: e.tensor_copy(out, in_), waits=w,
                       rd=[in_], wr=[out])

    def mset(self, ap, v, w=(), eng="vector"):
        return self.ex(eng, lambda e: e.memset(ap, v), waits=w, wr=[ap])

    def dma(self, q, out, in_, w=(), stream=None):
        base = {"sync": "d_s", "scalar": "d_sc", "gpsimd": "d_g",
                "vector": "d_v"}[q]
        pub = base if stream is None else base + "_" + stream
        return self.ex(q, lambda e: e.dma_start(out=out, in_=in_), waits=w, pub=pub,
                       rd=[in_], wr=[out], ckp=False)

    def coll(self, kind, op, rg, in_ap, out_ap, w=()):
        return self.ex("gpsimd", lambda e: e.collective_compute(
            kind, op, replica_groups=rg, ins=[in_ap], outs=[out_ap]), waits=w,
            pub="gE", rd=[in_ap], wr=[out_ap], ckp=False)

    def din(self, name, shape, dtype=F32):
        self.D[name] = self.nc.dram_tensor(name, list(shape), dtype,
                                           kind="ExternalInput").ap()
    def dout(self, name, shape, dtype=F32):
        self.D[name] = self.nc.dram_tensor(name, list(shape), dtype,
                                           kind="ExternalOutput").ap()
    def dint(self, name, shape, dtype=F32):
        self.D[name] = self.nc.dram_tensor(name, list(shape), dtype,
                                           kind="Internal").ap()
    def sb(self, name, shape, dtype=F32, st=None):
        self.T[name] = (st or self.es).enter_context(
            self.nc.sbuf_tensor(name, list(shape), dtype))
        return self.T[name]

    def dbg(self, name, ap, tok, dtype=None):
        if name not in self.dbg_req:
            return
        nm = "dbg_" + name
        self.dout(nm, tuple(ap.shape), dtype or F32)
        self.dbg_out.append((nm, tuple(ap.shape)))
        self.ex("gpsimd", lambda e, d=self.D[nm], ap=ap: e.dma_start(out=d[:], in_=ap),
                waits=[tok], pub="d_g", rd=[ap], wr=[self.D[nm]])

    def finish(self):
        self.barrier()
        with self.nc.Block() as block:
            for e in self.ENGINES:
                lst = self.ops[e]
                def make(mylist):
                    def body(eng):
                        for f in mylist:
                            f(eng)
                    return body
                getattr(block, e)(make(lst))
        self.es.close()
        return self.nc


def build(stage=99, dbg_names=()):
    k = K(stage, dbg_names)
    D, T = k.D, k.T

    k.din("x", (TPC, DM)); k.din("xnext", (NW, DM))
    k.din("WpT", (DM, DPROJ)); k.din("WoT", (DI, DM))
    k.din("Wg", (IFF, DM)); k.din("Wu", (IFF, DM)); k.din("Wd", (DM, IFF))
    k.din("smalls", (8, 128))
    k.din("bias_bc", (128, 2 * H * N), BF16)
    k.din("wn_bc", (128, DI), BF16)
    k.din("wbc_bc", (128, 2 * N))
    k.din("idn_f", (128, 128)); k.din("idn_b", (128, 128), BF16)
    k.din("tribig", (128, 128), BF16)
    k.din("masks", (128, 16))
    k.din("rows3", (1, 384))
    k.din("e0mat", (128, 128))
    k.din("mska", (128, 128), BF16)
    k.din("mskb4", (128, 512), BF16)
    k.dout("out", (TPC, DM))
    k.dint("xn_spill", (DM, TPC))
    k.dint("y_sp", (DI, TPC), BF16)
    k.dint("gate_sp", (IFF, TPC)); k.dint("up_sp", (IFF, TPC))
    k.dint("t_sp", (IFF, TPC)); k.dint("u_sp", (IFF, TPC))
    k.dint("s_sp", (DM, TPC)); k.dint("WdTH", (IFF, DM), BF16)
    k.dint("ag1_in", (128, 36)); k.dint("ag1_out", (512, 36))
    k.dint("armax_in", (1, 8)); k.dint("armax_out", (1, 8))
    for w in range(NW):
        k.dint("ag2_in%d" % w, (128, 1040))
        k.dint("ag2_out%d" % w, (512, 1040))
    k.dint("ar23_in", (1, 8)); k.dint("ar23_out", (1, 8))
    k.dint("ar4_in", (1, 8)); k.dint("ar4_out", (1, 8))
    k.dint("ar5_in", (1, 8)); k.dint("ar5_out", (1, 8))

    k.sb("consts", (128, 128))
    k.sb("idnb", (128, 128), BF16)
    k.sb("trib", (128, 128), BF16)
    k.sb("biasbc", (128, 2 * H * N), BF16)
    k.sb("wnbc", (128, DI), BF16)
    k.sb("wbcbc", (128, 2 * N))
    k.sb("smalls_s", (8, 128))
    k.sb("masks_s", (128, 16))
    k.sb("rows3_s", (1, 384))
    k.sb("e0mat_s", (128, 128))
    k.sb("mska_s", (128, 128), BF16)
    k.sb("mskb4_s", (128, 512), BF16)
    k.sb("small2", (32, 8))
    k.sb("Drow", (128, 32))
    k.sb("ones_row", (1, 128))
    k.sb("scal", (128, 16))
    k.sb("cvec", (128, 4))
    k.sb("csum", (128, 32))
    k.sb("maxp", (128, 4))
    k.sb("klb", (32, 8))
    k.sb("carryT", (128, 32))
    k.sb("scr8", (128, 16))
    psb = [k.es.enter_context(k.nc.psum_tensor("pb%d" % i, [128, 512], F32))
           for i in range(8)]
    k.psb = psb

    ph0(k, psb)
    if k.stage >= 1:
        ph1(k, psb)
    if k.stage >= 2:
        ph2(k, psb)
    if k.stage >= 3:
        ph3(k, psb)
    if k.stage >= 4:
        ph4(k, psb)
    if k.stage >= 5:
        ph5(k, psb)
    if k.stage >= 6:
        ph6(k, psb)
    if k.stage >= 7:
        ph7(k, psb)
    if k.stage >= 8:
        ph8(k, psb)
    if k.stage >= 9:
        ph9(k, psb)
    nc = k.finish()
    return nc, k.dbg_out


def ph0(k, psb):
    D, T = k.D, k.T
    k.dint("WpbfH", (DM, DPROJ), BF16)
    k.mset(T["cvec"][:, 0:1], EPS)
    k.mset(T["cvec"][:, 1:2], 1.0)
    tok_c = None
    for (dn, tn) in [("idn_f", "consts"), ("idn_b", "idnb"), ("tribig", "trib"),
                     ("bias_bc", "biasbc"), ("wn_bc", "wnbc"), ("wbc_bc", "wbcbc"),
                     ("smalls", "smalls_s"), ("masks", "masks_s"), ("rows3", "rows3_s"), ("e0mat", "e0mat_s"), ("mska", "mska_s"), ("mskb4", "mskb4_s")]:
        tok_c = k.dma("sync", T[tn][:], D[dn][:])
    k.tok_consts = tok_c
    with k.nc.sbuf_tensor("wstg", [128, 2 * DPROJ], F32) as wstg, \
            k.nc.sbuf_tensor("wbstg", [128, 2 * DPROJ], BF16) as wbstg:
        free_tok = [None, None]
        spill_tok = [None, None]
        for kc in range(8):
            b = kc % 2
            tl = k.dma("sync", wstg[:, b * DPROJ:(b + 1) * DPROJ],
                       D["WpT"][kc * 128:(kc + 1) * 128, :], w=[free_tok[b]],
                       stream="wp%d" % b)
            free_tok[b] = k.cp(wbstg[:, b * DPROJ:(b + 1) * DPROJ],
                               wstg[:, b * DPROJ:(b + 1) * DPROJ],
                               w=[tl, spill_tok[b]])
            spill_tok[b] = k.dma("scalar", D["WpbfH"][kc * 128:(kc + 1) * 128, :],
                                 wbstg[:, b * DPROJ:(b + 1) * DPROJ],
                                 w=[free_tok[b]], stream="wsp%d" % b)
        k.tok_wp = spill_tok[1]
        k.tok_wp0 = spill_tok[0]
        t_tr = k.tr(psb[0][0:128, 0:8], T["smalls_s"][:], w=[tok_c])
        k.ts(T["small2"][:, 0:1], psb[0][0:32, 1:2], -1.0, w=[t_tr])
        k.act(T["small2"][:, 1:2], psb[0][0:32, 2:3], AF.Exp, w=[t_tr])
        t_s2 = k.cp(T["small2"][:, 2:3], psb[0][0:32, 0:1], w=[t_tr])
        t_on = k.mset(T["ones_row"][:], 1.0)
        t_dm = k.mm(psb[1][:, 0:32], T["ones_row"][:], T["smalls_s"][0:1, 0:32],
                    w=[t_on, tok_c, t_tr])
        t_dr = k.cp(T["Drow"][:], psb[1][:, 0:32], w=[t_dm])
        k.tok_small2 = t_s2
        k.dbg("small2", T["small2"][:, 0:3], t_s2)
        k.dbg("Drow", T["Drow"][:], t_dr)
        k.barrier()


def ph1(k, psb):
    D, T = k.D, k.T
    with ExitStack() as st:
        xbuf = k.sb("xbuf", (128, 2 * DM), F32, st)
        sq = k.sb("sqt", (128, DM), F32, st)
        stat = k.sb("stat", (128, 16), F32, st)
        xn_tm = k.sb("xn_tm1", (128, DM), F32, st)
        ev = k.sb("ev_fp", (128, 2 * DM), F32, st)
        k.mset(T["maxp"][:, 0:1], 0.0)
        t0 = k.mset(T["csum"][:], 0.0)
        free_x = [None, None]
        ev_free = [None, None]
        last_tr = {}
        stat_free = None
        xn_free = None
        for tb in range(NTB):
            b = tb % 2
            xt = xbuf[:, b * DM:(b + 1) * DM]
            tl = k.dma("sync", xt, D["x"][tb * 128:(tb + 1) * 128, :], w=[free_x[b]],
                       stream="x%d" % b)
            k.red(stat[:, 0:1], xt, ALU.add, w=[tl, stat_free])
            k.tt(sq[:], xt, xt, ALU.mult, w=[tl])
            k.red(stat[:, 1:2], sq[:], ALU.add)
            k.ts(stat[:, 2:3], stat[:, 0:1], 1.0 / DM)
            k.tt(stat[:, 3:4], stat[:, 2:3], stat[:, 2:3], ALU.mult)
            k.ts(stat[:, 5:6], stat[:, 1:2], 1.0 / DM)
            tv = k.tt(stat[:, 6:7], stat[:, 5:6], stat[:, 3:4], ALU.subtract)
            k.act(stat[:, 7:8], stat[:, 6:7], AF.Ln, bias=T["cvec"][:, 0:1], w=[tv])
            tr_ = k.act(stat[:, 8:9], stat[:, 7:8], AF.Exp, scale=-0.5)
            stat_free = tr_
            k.ts(stat[:, 9:10], stat[:, 2:3], -1.0)
            tx = k.ts(xn_tm[:], xt, stat[:, 9:10], stat[:, 8:9], ALU.add, ALU.mult,
                      w=[tr_, xn_free])
            free_x[b] = tx
            ttr = []
            for fc in range(8):
                ttr.append(k.tr(psb[fc][:, 0:128], xn_tm[:, fc * 128:(fc + 1) * 128],
                                w=[tx, last_tr.get(fc)]))
            xn_free = ttr[-1]
            eb = tb % 2
            evs = []
            for fc in range(8):
                ta = k.cp(ev[:, eb * DM + fc * 128: eb * DM + (fc + 1) * 128],
                          psb[fc][:, 0:128],
                          w=[ttr[fc]] + (ev_free[eb] or []), eng="scalar")
                evs.append(ta)
                last_tr[fc] = ta
            tsp = k.dma("scalar",
                        D["xn_spill"].rearrange("(a p) t -> p a t", p=128)
                        [:, :, tb * 128:(tb + 1) * 128],
                        ev[:, eb * DM:(eb + 1) * DM].rearrange("p (a c) -> p a c", a=8),
                        w=[evs[-1]])
            w_ = tb // 4
            t_c1 = None
            for fc in range(8):
                t_c1 = k.red(T["scr8"][:, fc:fc + 1],
                             ev[:, eb * DM + fc * 128: eb * DM + (fc + 1) * 128],
                             ALU.add, w=[evs[fc], t0])
            t_c2 = k.tt(T["csum"][:, w_ * 8:(w_ + 1) * 8],
                        T["csum"][:, w_ * 8:(w_ + 1) * 8], T["scr8"][:, 0:8], ALU.add,
                        w=[t_c1])
            k.red(T["scr8"][:, 8:9], ev[:, eb * DM:(eb + 1) * DM], ALU.max, absv=True)
            t_m = k.tt(T["maxp"][:, 0:1], T["maxp"][:, 0:1], T["scr8"][:, 8:9], ALU.max)
            ev_free[eb] = [tsp, t_c1, t_m]
        k.dbg("stat", stat[:], stat_free)
        k.dbg("csum", T["csum"][:], t_c2)
        k.dbg("maxp", T["maxp"][:, 0:1], t_m)
        k.dbg("xn_last", xn_tm[:], t_m)
    k.barrier()


def ph2(k, psb):
    D, T = k.D, k.T
    with ExitStack() as st:
        xb = k.sb("xnb", (NW, DM), F32, st)
        stat = k.sb("stat2", (NW, 16), F32, st)
        sq = k.sb("sq2", (NW, DM), F32, st)
        xnn = k.sb("xnn", (NW, DM), F32, st)
        xnf = k.sb("xnf", (128, 8 * NW), F32, st)
        dl = k.sb("dlnext", (NW, 64), F32, st)
        dlT = k.sb("dlT", (64, NW), F32, st)
        tl = k.dma("sync", xb[:], D["xnext"][:])
        k.red(stat[:, 0:1], xb[:], ALU.add, w=[tl])
        k.tt(sq[:], xb[:], xb[:], ALU.mult, w=[tl])
        k.red(stat[:, 1:2], sq[:], ALU.add)
        k.ts(stat[:, 2:3], stat[:, 0:1], 1.0 / DM)
        k.tt(stat[:, 3:4], stat[:, 2:3], stat[:, 2:3], ALU.mult)
        k.ts(stat[:, 5:6], stat[:, 1:2], 1.0 / DM)
        tv = k.tt(stat[:, 6:7], stat[:, 5:6], stat[:, 3:4], ALU.subtract)
        k.act(stat[:, 7:8], stat[:, 6:7], AF.Ln, bias=T["cvec"][0:NW, 0:1], w=[tv])
        tr_ = k.act(stat[:, 8:9], stat[:, 7:8], AF.Exp, scale=-0.5)
        k.ts(stat[:, 9:10], stat[:, 2:3], -1.0)
        tx = k.ts(xnn[:], xb[:], stat[:, 9:10], stat[:, 8:9], ALU.add, ALU.mult,
                  w=[tr_])
        evs = []
        for fc in range(8):
            tt = k.tr(psb[fc][0:128, 0:NW], xnn[:, fc * 128:(fc + 1) * 128], w=[tx])
            evs.append(k.cp(xnf[:, fc * NW:(fc + 1) * NW], psb[fc][:, 0:NW], w=[tt],
                            eng="scalar"))
        wdl = k.sb("wdl", (128, 2 * 64), F32, st)
        mmtoks = []
        fwd = [None, None]
        for kc in range(8):
            b = kc % 2
            td1 = k.dma("sync", wdl[:, b * 64: b * 64 + 32],
                        D["WpT"][kc * 128:(kc + 1) * 128, OFF_DT:OFF_DT + 32],
                        w=[fwd[b]], stream="wdl%d" % b)
            td2 = k.dma("sync", wdl[:, b * 64 + 32: b * 64 + 64],
                        D["WpT"][kc * 128:(kc + 1) * 128, OFF_LAM:OFF_LAM + 32],
                        w=[fwd[b]], stream="wdl%d" % b)
            lhsT = xnf[:, kc * NW:(kc + 1) * NW]
            t2 = k.mm(psb[6][0:NW, 0:64], lhsT, wdl[:, b * 64:(b + 1) * 64],
                      start=(kc == 0), stop=(kc == 7),
                      w=evs + [td1, td2] if kc == 0 else [evs[kc], td1, td2])
            fwd[b] = t2
            mmtoks.append(t2)
        tev = k.cp(dl[:], psb[6][0:NW, 0:64], w=[mmtoks[-1]])
        ttr = k.tr(psb[7][0:64, 0:NW], dl[:], w=[tev])
        tevT = k.cp(dlT[:], psb[7][0:64, 0:NW], w=[ttr])
        sg = k.act(T["klb"][:, 0:NW], dlT[0:32, 0:NW], AF.Sigmoid,
                   bias=T["small2"][:, 0:1], scale=-1.0, w=[tevT, k.tok_small2])
        t1 = k.act(T["klb"][:, 4:4 + NW], T["klb"][:, 0:NW], AF.Ln)
        t2 = k.act(T["klb"][:, 0:NW], dlT[32:64, 0:NW], AF.Sigmoid)
        t3 = k.tt(T["klb"][:, 0:NW], T["klb"][:, 0:NW], T["klb"][:, 4:4 + NW], ALU.mult,
                  w=[t1, t2])
        t4 = k.tt(T["klb"][:, 0:NW], T["klb"][:, 0:NW], T["klb"][:, 4:4 + NW],
                  ALU.subtract, w=[t3])
        t5 = k.tt(T["klb"][:, 0:NW], T["klb"][:, 0:NW], T["masks_s"][0:32, 4:8], ALU.mult,
                  w=[k.tok_consts, t4])
        k.dbg("dl", dl[:], tev)
        k.dbg("dlT", dlT[:], tevT)
        k.dbg("xnn", xnn[:], tx)
        k.dbg("klb", T["klb"][:, 0:NW], t5)
        k.tok_klb = t5
    k.barrier()


def ph3(k, psb):
    D, T = k.D, k.T
    with ExitStack() as st:
        g1 = k.sb("g1", (128, 4 * 36), F32, st)
        asm = k.sb("asm1", (128, 36), F32, st)
        t0 = k.mset(asm[:, 32:36], 0.0)
        t1 = k.cp(asm[:, 0:32], T["csum"][:])
        t2 = k.cp(asm[:, 32:33], T["maxp"][:, 0:1], w=[t0])
        td = k.dma("gpsimd", D["ag1_in"][:], asm[:, 0:36], w=[t1, t2])
        tc1 = k.coll("AllGather", ALU.bypass, RG_SEQ, D["ag1_in"][:],
                     D["ag1_out"][:], w=[td])
        tg = k.dma("gpsimd", g1[:].rearrange("p (j c) -> p j c", j=4),
                   D["ag1_out"].rearrange("(j p) c -> p j c", p=128), w=[tc1])
        mx = k.sb("mx1", (128, 4), F32, st)
        toks = [k.cp(mx[:, j:j + 1], g1[:, j * 36 + 32: j * 36 + 33], w=[tg])
                for j in range(4)]
        tmx = k.red(mx[:, 0:1], mx[:, 0:4], ALU.max, w=toks)
        ttr = k.tr(psb[7][0:1, 0:128], mx[:, 0:1], w=[tmx])
        tz = k.mset(T["scr8"][0:1, 1:9], 0.0)
        tm2 = k.red(T["scr8"][0:1, 1:2], psb[7][0:1, 0:128], ALU.max, w=[ttr, tz])
        tfull = k.dma("gpsimd", D["armax_in"][:], T["scr8"][0:1, 1:9], w=[tm2])
        tc2 = k.coll("AllReduce", ALU.max, RG_ALL, D["armax_in"][:],
                     D["armax_out"][:], w=[tfull])
        ts1 = k.dma("gpsimd", T["scr8"][0:1, 1:9], D["armax_out"][:], w=[tc2])
        tbc = k.mm(psb[7][:, 0:1], T["ones_row"][:], T["scr8"][0:1, 1:2], w=[ts1])
        tsc = k.cp(T["scal"][:, 1:2], psb[7][:, 0:1], w=[tbc])
        tq = k.ts(T["scal"][:, 3:4], T["scal"][:, 1:2], 1.0 / 32767.0, 1e-12,
                  ALU.mult, ALU.max, w=[tsc])
        tqi = k.ex("vector", lambda e: e.reciprocal(T["scal"][:, 2:3], T["scal"][:, 3:4]),
                   waits=[tq])
        k.tok_s1 = tqi
        k.dbg("s1", T["scal"][0:1, 1:4], tqi)
        cr = k.sb("carry_raw", (128, 32), F32, st)
        R1 = k.sb("R1", (128, 8), F32, st)
        tR = k.mset(R1[:], 0.0)
        tcr = None
        for w_ in range(NW):
            sl = cr[:, w_ * 8:(w_ + 1) * 8]
            tA = k.cp(sl, R1[:], w=[tR])
            for j in range(4):
                gj = g1[:, j * 36 + w_ * 8: j * 36 + (w_ + 1) * 8]
                tA = k.stt(sl, gj, T["masks_s"][:, j:j + 1], sl,
                           ALU.mult, ALU.add, w=[tg, tA, k.tok_consts])
                tR = k.tt(R1[:], R1[:], gj, ALU.add, w=[tg, tR])
            tcr = tA
        wth = k.sb("wth", (128, 2 * 1024), F32, st)
        fw = [None, None]
        mmt = None
        THH = 1024
        for kc in range(8):
            b = kc % 2
            tl = k.dma("sync", wth[:, b * THH:(b + 1) * THH],
                       D["WpT"][kc * 128:(kc + 1) * 128, OFF_THETA:OFF_THETA + THH],
                       w=[fw[b]], stream="wth%d" % b)
            lT = cr.rearrange("p (a b) -> p a b", b=8)[:, :, kc]
            m1 = k.mm(psb[6][0:NW, 0:512], lT, wth[:, b * THH: b * THH + 512],
                      start=(kc == 0), stop=(kc == 7), w=[tl, tcr])
            m2 = k.mm(psb[7][0:NW, 0:512], lT, wth[:, b * THH + 512: b * THH + 1024],
                      start=(kc == 0), stop=(kc == 7))
            fw[b] = m2
            mmt = m2
        k.dbg("carryraw", cr[:], tcr)
        cth = k.sb("cth", (NW, 1024), F32, st)
        te1 = k.cp(cth[:, 0:512], psb[6][0:NW, 0:512], w=[mmt])
        te2 = k.cp(cth[:, 512:1024], psb[7][0:NW, 0:512], w=[mmt])
        k.dbg("cth", cth[:], te2)
        prev_cp = {0: None, 1: None}
        tlast = None
        for fc in range(8):
            bk = 4 + (fc % 2)
            tt = k.tr(psb[bk][0:128, 0:NW], cth[:, fc * 128:(fc + 1) * 128],
                      w=[te1, te2, prev_cp[fc % 2]])
            for w_ in range(NW):
                tlast = k.cp(T["carryT"][:, w_ * 8 + fc: w_ * 8 + fc + 1],
                             psb[bk][0:128, w_:w_ + 1], w=[tt])
            prev_cp[fc % 2] = tlast
        k.dbg("carryT", T["carryT"][:], tlast)
    k.barrier()





def tc_slice(thw, fc, c0=None, n=None):
    base = 4096 if fc == 0 else (fc - 1) * 512
    if c0 is None:
        return thw[:, base:base + 512]
    return thw[:, base + c0: base + c0 + n]


def ph4(k, psb):
    D, T = k.D, k.T
    k.dint("ygp_sp", (TPC, DI))
    k.barrier()

    st = ExitStack()
    Rw = k.sb("Rw", (128, 1024), F32, st)
    vown = k.sb("vown", (128, 1024), F32, st)
    vown_bf = k.sb("vown_bf", (128, 1024), BF16, st)
    VLq = [k.sb("VL%d" % i, (128, 1024), F32, st) for i in range(2)]
    VLp_bf = k.sb("VLp_bf", (128, 1024), BF16, st)
    Sgt = k.sb("Sgt", (128, 1040), F32, st)
    cum = k.sb("cum", (128, 16), F32, st)
    apair = k.sb("apair", (128, 16), F32, st)
    xnw_bf = k.sb("xnw_bf", (128, 8 * SEG), BF16, st)
    AR1 = k.sb("AR1", (128, 4096), F32, st)
    wgs = k.sb("wgs", (128, 2 * 512), BF16, st)
    wfs = k.sb("wfs", (128, 2 * 128), BF16, st)
    thw = k.sb("thw", (128, 4608), F32, st)
    VEC1 = k.sb("VEC1", (128, 512), F32, st)
    VEC2 = k.sb("VEC2", (128, 512), F32, st)
    kltbuf = k.sb("klt", (96, SEG + 16 + 512), F32, st)
    klt = kltbuf[64:96, 0:SEG + 16]
    sten = k.sb("sten", (32, 16), F32, st)
    arow = k.sb("arow", (1, 32), F32, st)
    lbig = k.sb("lbig", (128, 1024), F32, st)
    sz = k.sb("szw", (128, 4 * DI), BF16, st)
    xh_bf = k.sb("xhbf", (128, 4 * DI), BF16, st)
    bc4 = k.sb("bc4", (128, 4 * 128), F32, st)
    bcn = k.sb("bcn", (128, 128), F32, st)
    CmeD = k.sb("CmeDw", (128, 4 * DI), BF16, st)
    tctm = k.sb("tctm", (128, 1024), F32, st)
    sinb = k.sb("sinb", (128, 1024), BF16, st)
    cosb = k.sb("cosb", (128, 1024), BF16, st)
    B2b = k.sb("B2b", (128, 4096), BF16, st)
    Bmt = k.sb("Bmt", (128, 2048), BF16, st)
    Cmt = k.sb("Cmt", (128, 2048), BF16, st)
    fmB = k.sb("fmB", (128, 2048), BF16, st)
    fmC = k.sb("fmC", (128, 2048), BF16, st)
    fmE = k.sb("fmE", (128, 2048), BF16, st)
    Ft = k.sb("Ftl", (128, 2048), BF16, st)
    Gb = k.sb("Gbt", (128, 2048), BF16, st)
    d0t = k.sb("d0t", (128, 32), F32, st)
    ktm = k.sb("ktm", (128, 32), F32, st)
    b1tm = k.sb("b1tm", (128, 32), F32, st)
    eLt = k.sb("eLt", (128, 32), F32, st)
    ctm = k.sb("ctm", (128, 32), F32, st)
    wal = k.sb("wal", (128, 32), F32, st)
    s1a = k.sb("s1a", (128, 1024), F32, st)
    s1b = k.sb("s1b", (128, 1024), F32, st)
    ysp_stage = k.sb("ysp_stage", (128, 256), BF16, st)
    rmss = k.sb("rmss", (128, 8), F32, st)

    dtr = VEC1[0:32, :]
    bias1 = VEC1[32:64, :]
    mdt = VEC1[64:96, :]
    lamt = kltbuf[64:96, SEG + 16:SEG + 16 + 512]
    delt = VEC2[0:32, :]
    ellt = VEC2[32:64, :]
    gtil = VEC2[64:96, :]
    lamr = VEC2[96:128, :]
    Bmc = B2b[:, 0:2048]
    Cme = B2b[:, 2048:4096]
    ygp = AR1[:, 0:2048]
    ywb = s1a[:, 0:1024].bitcast(BF16)

    k.mset(Rw[:], 0.0)
    k.mset(lbig[:], 0.0)
    pv = {}
    tok_ysp_all = []

    for w in range(NW):
        t_ld = k.dma("sync", AR1[:], D["xn_spill"].rearrange(
            "(a p) t -> p a t", p=128)[:, :, w * SEG:(w + 1) * SEG],
            w=[pv.get("AR1")], stream="xnw")
        t_cast = k.cp(xnw_bf[:], AR1[:], w=[t_ld, pv.get("xnwbf")])
        fbs = [("dt", OFF_DT, 32), ("lam", OFF_LAM, 32)] + \
            [("th%d" % j, OFF_THETA + 128 * j, 128) for j in range(8)]
        fw = [None, None]
        last_bk = {}
        ev_fm = []
        tscn = [None] * 8
        for fi, (nmf, col0, wid) in enumerate(fbs):
            bk = 5 + (fi % 3)
            bb = fi % 2
            mmt = None
            for kc in range(8):
                tl = k.dma("sync", wfs[:, bb * 128: bb * 128 + wid],
                           D["WpbfH"][kc * 128:(kc + 1) * 128, col0:col0 + wid],
                           w=[fw[bb], k.tok_wp, k.tok_wp0], stream="wf%d" % bb)
                mmt = k.mm(psb[bk][0:wid, 0:SEG],
                           wfs[:, bb * 128: bb * 128 + wid],
                           xnw_bf[:, kc * SEG:(kc + 1) * SEG],
                           start=(kc == 0), stop=(kc == 7),
                           w=[tl, t_cast, last_bk.get(bk), pv.get("pb%d" % bk)])
                fw[bb] = mmt
            if nmf == "dt":
                ev = k.cp(dtr, psb[bk][0:32, 0:SEG], w=[mmt, pv.get("vecf")],
                          eng="scalar")
            elif nmf == "lam":
                ev = k.cp(lamr, psb[bk][0:32, 0:SEG], w=[mmt, pv.get("vecf")],
                          eng="scalar")
            else:
                j = int(nmf[2:])
                ev = k.cp(thw[:, j * 512:(j + 1) * 512], psb[bk][0:128, 0:SEG],
                          w=[mmt, pv.get("thwf")], eng="scalar")
                tscn[j] = k.scan(tc_slice(thw, j),
                                 thw[:, j * 512:(j + 1) * 512],
                                 thw[:, j * 512:(j + 1) * 512],
                                 T["carryT"][:, w * 8 + j: w * 8 + j + 1],
                                 ALU.add, ALU.bypass,
                                 w=[ev, (tscn[j - 1] if j >= 1 else None),
                                    pv.get("thwf")])
            last_bk[bk] = ev
            ev_fm.append(ev)
        t1 = k.act(mdt, dtr, AF.Sigmoid, bias=T["small2"][:, 0:1],
                   scale=-1.0, w=ev_fm + [k.tok_small2])
        t1 = k.act(mdt, mdt, AF.Ln)
        t2 = k.act(lamt, lamr, AF.Sigmoid, w=ev_fm)
        t3 = k.ts(delt, mdt, T["small2"][:, 1:2], None, ALU.mult, w=[t1])
        t4 = k.scan(ellt, delt, delt, 0.0, ALU.add, ALU.bypass, w=[t3])
        t5 = k.tt(klt[:, 0:SEG], lamt, mdt, ALU.mult, w=[t1, t2])
        t6 = k.tt(klt[:, 0:SEG], klt[:, 0:SEG], mdt, ALU.subtract, w=[t5])
        t6b = k.cp(klt[:, SEG:SEG + 1], T["klb"][:, w:w + 1], w=[k.tok_klb])
        tg1 = k.tt(gtil, lamt, mdt, ALU.mult, w=[t1, t2])
        tg2 = k.tt(gtil, klt[:, 1:SEG + 1], gtil, ALU.subtract, w=[t6, t6b, tg1])
        tg3 = k.act(bias1, gtil, AF.Ln, w=[tg2])
        tg4 = k.tt(bias1, bias1, ellt, ALU.subtract, w=[t4, tg3])
        fw2 = [None, None]
        bank_free = {}
        ev_tm = []
        for og in range(9):
            if og < 4:
                col0, wid = og * 512, 512
            elif og < 8:
                col0, wid = 2048 + (og - 4) * 512, 512
            else:
                col0, wid = 4096, 128
            banks = [(og % 2) * 4 + i for i in range(4)]
            mmt = {}
            for kc in range(8):
                bb = og % 2
                tl = k.dma("sync", wgs[:, bb * 512: bb * 512 + wid],
                           D["WpbfH"][kc * 128:(kc + 1) * 128, col0:col0 + wid],
                           w=[fw2[bb]], stream="wg%d" % bb)
                for tb in range(4):
                    wts = [tl, t_cast]
                    if kc == 0:
                        wts += [bank_free.get(banks[tb]), pv.get("pb%d" % banks[tb]),
                                last_bk.get(banks[tb])]
                    mmt[tb] = k.mm(
                        psb[banks[tb]][0:128, 0:wid],
                        xnw_bf[:, kc * SEG + tb * 128: kc * SEG + (tb + 1) * 128],
                        wgs[:, bb * 512: bb * 512 + wid],
                        start=(kc == 0), stop=(kc == 7), w=wts)
                fw2[bb] = mmt[3]
            for tb in range(4):
                bk = banks[tb]
                if og < 4:
                    ev = k.act(sz[:, tb * DI + og * 512: tb * DI + og * 512 + 512],
                               psb[bk][0:128, 0:512], AF.Silu,
                               w=[mmt[tb], pv.get("szf")])
                elif og < 8:
                    ev = k.cp(xh_bf[:, tb * DI + (og - 4) * 512:
                                    tb * DI + (og - 4) * 512 + 512],
                              psb[bk][0:128, 0:512], w=[mmt[tb], pv.get("xhf")])
                else:
                    ev = k.cp(bc4[:, tb * 128:(tb + 1) * 128],
                              psb[bk][0:128, 0:128], w=[mmt[tb], pv.get("bcf")])
                bank_free[bk] = ev
                ev_tm.append(ev)
        t_cum = k.mset(cum[:], 1.0, w=[pv.get("cumf")])
        t_va = k.mset(vown[:], 0.0, w=[pv.get("vownf")])
        pb_free = dict(bank_free)
        ygp_w = []
        for c4 in range(4):
            ch0 = c4 * 128
            VLold = VLq[c4 % 2]
            VLnew = VLq[1 - c4 % 2]
            bcs = bc4[:, c4 * 128:(c4 + 1) * 128]
            tq = k.tt(bcn[:], bcs, bcs, ALU.mult, w=ev_tm)
            tq = k.red(rmss[:, 0:2], bcn[:].rearrange("p (a c) -> p a c", a=2),
                       ALU.add, w=[tq])
            tq = k.act(rmss[:, 2:4], rmss[:, 0:2], AF.Ln, bias=T["cvec"][:, 0:1],
                       scale=1.0 / 64.0, w=[tq])
            tq = k.act(rmss[:, 4:6], rmss[:, 2:4], AF.Exp, scale=-0.5, w=[tq])
            tb1 = k.ts(bcn[:, 0:64], bcs[:, 0:64], rmss[:, 4:5], None, ALU.mult,
                       w=[tq])
            tb2 = k.ts(bcn[:, 64:128], bcs[:, 64:128], rmss[:, 5:6], None,
                       ALU.mult, w=[tq])
            tb3 = k.tt(bcn[:], bcn[:], T["wbcbc"][:, 0:128], ALU.mult,
                       w=[tb1, tb2, k.tok_consts])
            tB2 = k.tt(AR1[:, 0:2048].rearrange("p (h n) -> p h n", h=32),
                       bcn[:, 0:64].unsqueeze(1).broadcast_to((128, 32, 64)),
                       T["biasbc"][:, 0:2048].rearrange("p (h n) -> p h n", h=32),
                       ALU.add, w=[tb3, t_cast, pv.get("ygpf")])
            tC2 = k.tt(AR1[:, 2048:4096].rearrange("p (h n) -> p h n", h=32),
                       bcn[:, 64:128].unsqueeze(1).broadcast_to((128, 32, 64)),
                       T["biasbc"][:, 2048:4096].rearrange("p (h n) -> p h n", h=32),
                       ALU.add, w=[tb3])
            tP1 = k.tt(s1a[:], AR1[:, 0:1024], AR1[:, 2048:3072], ALU.mult,
                       w=[tB2, tC2, pv.get("s1af")])
            tP2 = k.tt(s1b[:], AR1[:, 1024:2048], AR1[:, 3072:4096], ALU.mult,
                       w=[tB2, tC2, pv.get("s1bf")])
            td0a = k.red(d0t[:, 0:16], s1a[:].rearrange("p (h n) -> p h n", h=16),
                         ALU.add, w=[tP1])
            td0 = k.red(d0t[:, 16:32], s1b[:].rearrange("p (h n) -> p h n", h=16),
                        ALU.add, w=[tP2])
            tBb = k.cp(B2b[:, 0:2048], AR1[:, 0:2048],
                       w=[tB2, pv.get("B2bf"), pv.get("B2bfE")])
            tCb = k.cp(B2b[:, 2048:4096], AR1[:, 2048:4096],
                       w=[tC2, pv.get("B2bf"), pv.get("B2bfE")])
            ttc = []
            for fc in range(8):
                bkp = fc % 2
                tt0 = k.tr(psb[bkp][0:128, 0:128], tc_slice(thw, fc, ch0, 128),
                           w=[tscn[fc], (tscn[fc + 1] if fc + 1 < 8 else None),
                              pb_free.get(bkp),
                              ttc[fc - 2] if fc >= 2 else None])
                ttc.append(k.cp(tctm[:, fc * 128:(fc + 1) * 128],
                                psb[bkp][0:128, 0:128], w=[tt0], eng="scalar"))
            pb_free[0] = ttc[-2]
            pb_free[1] = ttc[-1]
            PIF = float(np.pi)
            ta1 = k.ts(s1a[:].bitcast(I32), tctm[:], INV2PI, 128.5, ALU.mult,
                       ALU.add, w=ttc + [td0a, td0])
            ta2 = k.ts(s1b[:], s1a[:].bitcast(I32), 1.0, -128.0, ALU.mult,
                       ALU.add, w=[ta1, tP2])
            ta2b = k.ts(s1b[:], s1b[:], -TWOPI, None, ALU.mult)
            ta3 = k.tt(s1b[:], s1b[:], tctm[:], ALU.add, eng="gpsimd", w=[ta2b])
            ta3m = k.ts(s1a[:], s1b[:], -PIF, None, ALU.is_lt, w=[ta3])
            ta3n = k.stt(s1b[:], s1a[:], TWOPI, s1b[:], ALU.mult, ALU.add,
                         w=[ta3m])
            ta3b = k.ts(s1b[:], s1b[:], -PIF, PIF, ALU.max, ALU.min, w=[ta3n])
            ts1 = k.act(sinb[:], s1b[:], AF.Sin, w=[ta3b])
            ta4 = k.ts(s1a[:].bitcast(I32), tctm[:], INV2PI, 128.75, ALU.mult,
                       ALU.add, w=[ta2])
            ta5 = k.ts(s1b[:], s1a[:].bitcast(I32), 1.0, -128.0, ALU.mult,
                       ALU.add, w=[ta4, ts1, ta3b])
            ta5b = k.ts(s1b[:], s1b[:], -TWOPI, HALFPI, ALU.mult, ALU.add)
            ta6 = k.tt(s1b[:], s1b[:], tctm[:], ALU.add, eng="gpsimd", w=[ta5b])
            ta6m = k.ts(s1a[:], s1b[:], -PIF, None, ALU.is_lt, w=[ta6])
            ta6n = k.stt(s1b[:], s1a[:], TWOPI, s1b[:], ALU.mult, ALU.add,
                         w=[ta6m])
            ta6b = k.ts(s1b[:], s1b[:], -PIF, PIF, ALU.max, ALU.min, w=[ta6n])
            ts2 = k.act(cosb[:], s1b[:], AF.Sin, w=[ta6b])
            ta5 = ta5b
            ta6 = ta6b
            def rope(dst, off, wt):
                v1 = B2b[:, off:off + 2048].rearrange("p (h n) -> p h n",
                                                      h=32)[:, :, 0:32]
                v2 = B2b[:, off:off + 2048].rearrange("p (h n) -> p h n",
                                                      h=32)[:, :, 32:64]
                cs = cosb[:].rearrange("p (h f) -> p h f", h=32)
                sn = sinb[:].rearrange("p (h f) -> p h f", h=32)
                d3 = dst.rearrange("p (h n) -> p h n", h=32)
                sA = s1a[:].rearrange("p (h f) -> p h f", h=32)
                sB = s1b[:].rearrange("p (h f) -> p h f", h=32)
                r1 = k.tt(d3[:, :, 0:32], v1, cs, ALU.mult, w=[ts1, ts2, wt])
                r2 = k.tt(sA, v2, sn, ALU.mult, eng="gpsimd",
                          w=[ts1, ts2, wt, ta5])
                r3 = k.tt(d3[:, :, 0:32], d3[:, :, 0:32], sA, ALU.subtract,
                          w=[r1, r2])
                r4 = k.tt(sB, v1, sn, ALU.mult, eng="gpsimd", w=[ts1, ts2, wt, ta6])
                r5 = k.tt(d3[:, :, 32:64], v2, cs, ALU.mult, w=[ts1, ts2, wt])
                r6 = k.tt(d3[:, :, 32:64], d3[:, :, 32:64], sB, ALU.add,
                          w=[r4, r5])
                return r6
            tBm = rope(Bmt[:], 0, tBb)
            tCm = rope(Cmt[:], 2048, tCb)
            if ch0 > 0:
                tprev = k.ts(sten[:, 0:1], ellt[:, ch0 - 1:ch0], -1.0, None,
                             ALU.mult, w=[t4])
            else:
                tprev = k.mset(sten[:, 0:1], 0.0, w=[t4])
            teL = k.act(s1a[0:32, 0:128], ellt[:, ch0:ch0 + 128], AF.Exp,
                        bias=sten[:, 0:1], w=[tprev, tBm, tCm])
            ttE = k.tr(psb[2][0:128, 0:32], s1a[0:32, 0:128],
                       w=[teL, pb_free.get(2), pv.get("pb2")])
            teL2 = k.cp(eLt[:], psb[2][0:128, 0:32], w=[ttE])
            ttA = k.tr(psb[3][0:1, 0:32], s1a[0:32, 127:128],
                       w=[teL, pb_free.get(3), pv.get("pb3")])
            tAr = k.cp(arow[:], psb[3][0:1, 0:32], w=[ttA])
            tcf = k.act(s1a[0:32, 128:256], bias1[:, ch0:ch0 + 128], AF.Exp,
                        bias=ellt[:, ch0 + 127:ch0 + 128], w=[tg4, teL])
            tct = k.tr(psb[2][0:128, 32:64], s1a[0:32, 128:256], w=[tcf, ttE])
            tct2 = k.cp(ctm[:], psb[2][0:128, 32:64], w=[tct])
            tks = k.tr(psb[3][0:128, 32:64], klt[:, ch0 + 1:ch0 + 129],
                       w=[t6, t6b, ttA])
            tkt = k.cp(ktm[:], psb[3][0:128, 32:64], w=[tks])
            tbt = k.tr(psb[2][0:128, 64:96], VEC1[32:64, ch0:ch0 + 128],
                       w=[tg4, tct])
            tbt2 = k.cp(b1tm[:], psb[2][0:128, 64:96], w=[tbt])
            pb_free[2] = tbt2
            tw1 = k.tt(wal[:], ktm[:], d0t[:], ALU.mult, w=[tkt, td0, td0a])
            tw2 = k.tt(wal[:], T["Drow"][:], wal[:], ALU.subtract, w=[tw1])
            tCme = k.tt(Cme.rearrange("p (h n) -> p h n", h=32),
                        Cmt[:].rearrange("p (h n) -> p h n", h=32),
                        eLt[:].unsqueeze(2).broadcast_to((128, 32, 64)),
                        ALU.mult, w=[tCm, teL2, tCb])
            tBmc = k.tt(Bmc.rearrange("p (h n) -> p h n", h=32),
                        Bmt[:].rearrange("p (h n) -> p h n", h=32),
                        ctm[:].unsqueeze(2).broadcast_to((128, 32, 64)),
                        ALU.mult, w=[tBm, tct2, tBb])
            prev_fm = {}
            tok_fm = {}
            for (nm, srct, dstt, wt) in [("B", Bmt[:], fmB, tBm),
                                         ("C", Cmt[:], fmC, tCm),
                                         ("E", Cme, fmE, tCme)]:
                evl = None
                for pr in range(16):
                    bkp = 6 + pr % 2
                    pbv = psb[bkp][:].bitcast(BF16)[0:128, 0:128]
                    tt0 = k.tr(pbv, srct[:, pr * 128:(pr + 1) * 128], T["idnb"][:],
                               w=[wt, prev_fm.get(bkp), pb_free.get(bkp)])
                    if pr < 2:
                        pb_free[bkp] = None
                    evl = k.cp(dstt[:, pr * 128:(pr + 1) * 128],
                               pbv, w=[tt0], eng="scalar")
                    prev_fm[bkp] = evl
                tok_fm[nm] = evl
            pb_free[6] = prev_fm[6]
            pb_free[7] = prev_fm[7]
            tcd = None
            for pr in range(16):
                tcd = k.ts(CmeD[:, c4 * DI + pr * 128: c4 * DI + (pr + 1) * 128],
                           fmE[:, pr * 128:(pr + 1) * 128],
                           cum[:, pr:pr + 1], None, ALU.mult,
                           w=[tok_fm["E"], t_cum, pv.get("cmedf")])
            arv = arow[:].rearrange("p (a b) -> p a b", b=2)
            tap1 = k.mm(psb[3][0:128, 64:80], T["rows3_s"][0:1, 128:256],
                        arv[:, :, 0], start=True, stop=False,
                        w=[tAr, tks, k.tok_consts])
            tap2 = k.mm(psb[3][0:128, 64:80], T["rows3_s"][0:1, 256:384],
                        arv[:, :, 1], start=False, stop=True)
            tap3 = k.cp(apair[:], psb[3][0:128, 64:80], w=[tap2])
            pb_free[3] = tap3
            pv["pb3"] = None
            smm = []
            for pr in range(16):
                bkS = 4 + pr // 8
                colS = (pr % 8) * 64
                wts = [tBmc]
                if pr % 8 == 0:
                    wts += [pb_free.get(bkS), pv.get("pb%d" % bkS)]
                    pb_free[bkS] = None
                    pv["pb%d" % bkS] = None
                m1 = k.mm(psb[bkS][0:64, colS:colS + 64],
                          Bmc[:, pr * 128: pr * 128 + 64],
                          xh_bf[:, c4 * DI + pr * 128: c4 * DI + pr * 128 + 64],
                          tp=(0, 0), w=wts)
                m2 = k.mm(psb[bkS][64:128, colS:colS + 64],
                          Bmc[:, pr * 128 + 64: pr * 128 + 128],
                          xh_bf[:, c4 * DI + pr * 128 + 64: c4 * DI + pr * 128 + 128],
                          tp=(0, 64))
                smm.append(m2)
            if c4 == 0:
                tVb = k.mset(VLp_bf[:], 0.0, w=[pv.get("VLbf")])
            else:
                tVb = k.cp(VLp_bf[:], VLold[:], w=[pv.get("VLbf")])
            tvl = []
            for pr in range(16):
                bkS = 4 + pr // 8
                colS = (pr % 8) * 64
                half = psb[bkS][0:128, colS:colS + 64]
                if c4 == 0:
                    tvl.append(k.cp(VLnew[:, pr * 64:(pr + 1) * 64], half,
                                    w=[smm[pr], pv.get("VLfree")]))
                else:
                    tvl.append(k.stt(VLnew[:, pr * 64:(pr + 1) * 64],
                                     VLold[:, pr * 64:(pr + 1) * 64],
                                     apair[:, pr:pr + 1], half,
                                     ALU.mult, ALU.add,
                                     w=[smm[pr], tap3, pv.get("VLfree")]))
            t_cum = k.tt(cum[:], cum[:], apair[:], ALU.mult, w=[tap3, tcd])
            evy_all = []
            tlb_prev = pv.get("lbrf")
            gmm = {}
            dmm = {}
            for h2 in range(2):
                hh0 = h2 * 16
                for g8 in range(2):
                    h0 = hh0 + g8 * 8
                    tlb = k.dma("gpsimd", lbig[0:1, 0:1024],
                                ellt[h0:h0 + 8, ch0:ch0 + 128],
                                w=[t4, tlb_prev], stream="lbr")
                    for qi in range(8):
                        q = g8 * 8 + qi
                        h = hh0 + q
                        pr, half = h // 2, h % 2
                        bkG = q // 4
                        colG = (q % 4) * 128
                        bkD = 4 + q // 4
                        prt = slice(half * 64, half * 64 + 64)
                        wtsG = [tok_fm["B"], tok_fm["C"]]
                        if q % 4 == 0:
                            wtsG += [pb_free.get(bkG), tvl[-1], ttc[2 * bkG]]
                            pb_free[bkG] = None
                        gmm[q] = k.mm(psb[bkG][0:128, colG:colG + 128],
                                      fmB[:, pr * 128:(pr + 1) * 128][prt, :],
                                      fmC[:, pr * 128:(pr + 1) * 128][prt, :],
                                      tp=(64 * half, 0), w=wtsG)
                        wtsD = [tlb]
                        if q % 4 == 0:
                            wtsD += [pb_free.get(bkD), tvl[-1]]
                            pb_free[bkD] = None
                        dmm[q] = k.mm(psb[bkD][0:128, colG:colG + 128],
                                      T["e0mat_s"][:],
                                      lbig[:, qi * 128:(qi + 1) * 128],
                                      start=(q % 4 == 0), stop=False, w=wtsD)
                    for q in range(g8 * 8, g8 * 8 + 8, 4):
                        bkD = 4 + q // 4
                        colG0 = (q % 4) * 128
                        dmm[q + 3] = k.mm(psb[bkD][0:128, 0:512],
                                          T["mska_s"][:], T["mskb4_s"][:],
                                          start=False, stop=True,
                                          w=[k.tok_consts])
                    tlb_prev = dmm[g8 * 8 + 7]
                evG = []
                evF = {}
                for bk in range(4):
                    evG.append(k.cp(Gb[:, bk * 512:(bk + 1) * 512],
                                    psb[bk][0:128, 0:512],
                                    w=[gmm[hh0 % 16 + bk * 4 + 3] if False else
                                       gmm[bk * 4 + 3 + 0]], eng="scalar"))
                for q in range(16):
                    h = hh0 + q
                    bkD = 4 + q // 4
                    colG = (q % 4) * 128
                    evF[q] = k.act(Ft[:, q * 128:(q + 1) * 128],
                                   psb[bkD][0:128, colG:colG + 128], AF.Exp,
                                   bias=b1tm[:, h:h + 1],
                                   w=[dmm[(q // 4) * 4 + 3], tbt2])
                tW1 = k.tt(Ft[:], Ft[:], Gb[:], ALU.mult,
                            w=list(evF.values()) + evG)
                ymm = {}
                cmm = {}
                for q in range(16):
                    h = hh0 + q
                    pr, half = h // 2, h % 2
                    bkY = q // 8
                    colY = (q % 8) * 64
                    prt = slice(half * 64, half * 64 + 64)
                    ymm[q] = k.mm(psb[bkY][0:128, colY:colY + 64],
                                  Ft[:, q * 128:(q + 1) * 128],
                                  xh_bf[:, c4 * DI + h * 64: c4 * DI + h * 64 + 64],
                                  w=[tW1] + (evG if q % 8 == 0 else []))
                    cmm[q] = k.mm(psb[4 + bkY][0:128, colY:colY + 64],
                                  fmE[:, pr * 128:(pr + 1) * 128][prt, :],
                                  VLp_bf[prt, pr * 64:(pr + 1) * 64],
                                  tp=(64 * half, 0),
                                  w=[tok_fm["E"], tVb] +
                                  (list(evF.values()) if q % 8 == 0 else []))
                for bq in range(2):
                    c0 = h2 * 1024 + bq * 512
                    e1 = k.cp(s1a[:, bq * 512:(bq + 1) * 512],
                              psb[4 + bq][0:128, 0:512],
                              w=[cmm[bq * 8 + 7]], eng="scalar")
                    e2 = k.stt(s1b[:, bq * 512:(bq + 1) * 512],
                               psb[bq][0:128, 0:512], 0.0,
                               s1a[:, bq * 512:(bq + 1) * 512],
                               ALU.add, ALU.add, w=[ymm[bq * 8 + 7], e1])
                    hw0 = hh0 + bq * 8
                    e3 = k.tt(s1a[:, bq * 512:(bq + 1) * 512].rearrange(
                        "p (a c) -> p a c", a=8),
                        xh_bf[:, c4 * DI + hw0 * 64: c4 * DI + (hw0 + 8) * 64]
                        .rearrange("p (a c) -> p a c", a=8),
                        wal[:, hw0:hw0 + 8].unsqueeze(2).broadcast_to((128, 8, 64)),
                        ALU.mult, eng="gpsimd", w=[tw2, e2])
                    e4 = k.tt(s1b[:, bq * 512:(bq + 1) * 512],
                              s1b[:, bq * 512:(bq + 1) * 512],
                              s1a[:, bq * 512:(bq + 1) * 512], ALU.add, w=[e3, e2])
                    e5 = k.tt(ygp[:, c0:c0 + 512], s1b[:, bq * 512:(bq + 1) * 512],
                              sz[:, c4 * DI + c0: c4 * DI + c0 + 512], ALU.mult,
                              w=[e4, pv.get("ygpf"), tBb, tCb, tP1, tP2])
                    evy_all.append(e5)
                pb_free[0] = evy_all[-1]
                pb_free[1] = evy_all[-1]
                pb_free[2] = evG[2]
                pb_free[3] = evG[3]
                pb_free[4] = evy_all[-1]
                pb_free[5] = evy_all[-1]
                pv["pb2"] = None
            tok_sp = k.dma("scalar",
                           D["ygp_sp"][w * SEG + ch0: w * SEG + ch0 + 128, :],
                           ygp[:], w=evy_all, stream="ygp")
            ygp_w.append(tok_sp)
            pv["ygpf"] = tok_sp
            pv["VLfree"] = evy_all[-1]
            pv["VLbf"] = cmm[15]
            pv["s1af"] = evy_all[-1]
            pv["s1bf"] = evy_all[-1]
            pv["B2bf"] = tvl[-1]
            pv["B2bfE"] = tok_fm["E"]
            pv["lbrf"] = tlb_prev
        tpk1 = k.cp(Sgt[:, 0:1024], VLq[0][:], w=tvl)
        tpk2 = k.cp(Sgt[:, 1024:1040], cum[:], w=[t_cum])
        tdi = k.dma("gpsimd", D["ag2_in%d" % w][:], Sgt[:, 0:1040],
                    w=[tpk1, tpk2], stream="ag2")
        tco = k.coll("AllGather", ALU.bypass, RG_SEQ, D["ag2_in%d" % w][:],
                     D["ag2_out%d" % w][:], w=[tdi])
        tfold = None
        for j in range(4):
            tgj = k.dma("gpsimd", Sgt[:],
                        D["ag2_out%d" % w][j * 128:(j + 1) * 128, :],
                        w=[tco, tfold], stream="sgl")
            tc1 = k.stt(vown[:], Rw[:], T["masks_s"][:, 8 + j: 9 + j], vown[:],
                        ALU.mult, ALU.add, w=[tgj, t_va, k.tok_consts])
            tc2 = k.tt(s1a[:].rearrange("p (a c) -> p a c", a=16),
                       Rw[:].rearrange("p (a c) -> p a c", a=16),
                       Sgt[:, 1024:1040].unsqueeze(2).broadcast_to((128, 16, 64)),
                       ALU.mult, w=[tgj, tc1, pv.get("s1af")])
            tc3 = k.tt(Rw[:], s1a[:], Sgt[:, 0:1024], ALU.add, w=[tc2])
            pv["s1af"] = tc3
            tfold = tc3
        tvbf = k.cp(vown_bf[:], vown[:], w=[tfold])
        pv["vownf"] = tvbf
        ysp_last = {}
        for c4 in range(4):
            ch0 = c4 * 128
            cg = {}
            for q in range(32):
                pr, half = q // 2, q % 2
                bkC = q // 8
                colC = (q % 8) * 64
                prt = slice(half * 64, half * 64 + 64)
                wts = [tvbf, tcd]
                if q % 8 == 0:
                    wts += [pb_free.get(bkC), pv.get("pbc%d" % bkC)]
                    pb_free[bkC] = None
                cg[q] = k.mm(psb[bkC][0:128, colC:colC + 64],
                             CmeD[:, c4 * DI + pr * 128: c4 * DI + (pr + 1) * 128]
                             [prt, :],
                             vown_bf[prt, pr * 64:(pr + 1) * 64],
                             tp=(64 * half, 0), w=wts)
            tld = k.dma("sync", ygp[:],
                        D["ygp_sp"][w * SEG + ch0: w * SEG + ch0 + 128, :],
                        w=[ygp_w[c4], pv.get("ygprd"), pv.get("AR1f")],
                        stream="ygprd")
            ev4 = []
            for bkC in range(4):
                eA = k.tt(s1b[:, (bkC % 2) * 512:(bkC % 2) * 512 + 512],
                          psb[bkC][0:128, 0:512],
                          sz[:, c4 * DI + bkC * 512: c4 * DI + (bkC + 1) * 512],
                          ALU.mult, w=[cg[bkC * 8 + 7], pv.get("s1bf")])
                eB = k.tt(ygp[:, bkC * 512:(bkC + 1) * 512],
                          ygp[:, bkC * 512:(bkC + 1) * 512],
                          s1b[:, (bkC % 2) * 512:(bkC % 2) * 512 + 512],
                          ALU.add, w=[eA, tld])
                ev4.append(eB)
                pv["s1bf"] = eB
                pv["pbc%d" % bkC] = eB
            tsq = k.tt(s1b[:], ygp[:, 0:1024], ygp[:, 0:1024], ALU.mult, w=ev4)
            tr1 = k.red(rmss[:, 0:1], s1b[:], ALU.add, w=[tsq])
            tsq2 = k.tt(s1b[:], ygp[:, 1024:2048], ygp[:, 1024:2048], ALU.mult,
                        w=[tr1])
            tr2 = k.red(rmss[:, 1:2], s1b[:], ALU.add, w=[tsq2])
            tr3 = k.tt(rmss[:, 2:3], rmss[:, 0:1], rmss[:, 1:2], ALU.add,
                       w=[tr1, tr2])
            tr4 = k.act(rmss[:, 3:4], rmss[:, 2:3], AF.Ln, bias=T["cvec"][:, 0:1],
                        scale=1.0 / DI, w=[tr3])
            tr5 = k.act(rmss[:, 4:5], rmss[:, 3:4], AF.Exp, scale=-0.5, w=[tr4])
            ty1 = k.ts(ygp[:], ygp[:], rmss[:, 4:5], None, ALU.mult,
                       w=[tr5, tsq2])
            ty2 = k.tt(ywb, ygp[:], T["wnbc"][:], ALU.mult,
                       w=[ty1, k.tok_consts, pv.get("s1af")])
            pv["s1af"] = ty2
            pv["s1bf"] = ty2
            evsp = []
            for fc in range(16):
                bkp = 4 + fc % 2
                pbv = psb[bkp][:].bitcast(BF16)[0:128, 0:128]
                tt0 = k.tr(pbv, ywb[:, fc * 128:(fc + 1) * 128], T["idnb"][:],
                           w=[ty2, ysp_last.get(fc % 2), pb_free.get(bkp)])
                if fc < 2:
                    pb_free[bkp] = None
                ev = k.cp(ysp_stage[:, (fc % 2) * 128:(fc % 2) * 128 + 128],
                          pbv, w=[tt0, ysp_last.get(fc % 2)], eng="scalar")
                tsp2 = k.dma("scalar",
                             D["y_sp"][fc * 128:(fc + 1) * 128,
                                       w * SEG + ch0: w * SEG + ch0 + 128],
                             ysp_stage[:, (fc % 2) * 128:(fc % 2) * 128 + 128],
                             w=[ev], stream="ysp%d" % (fc % 2))
                ysp_last[fc % 2] = tsp2
                evsp.append(tsp2)
            pb_free[4] = evsp[-2]
            pb_free[5] = evsp[-1]
            pv["ygprd"] = ty2
            tok_ysp_all.extend(evsp[-2:])
        pv["AR1"] = ty2
        pv["AR1f"] = ty2
        pv["ygpf"] = ty2
        pv["vecf"] = tg4
        pv["thwf"] = ttc[-1]
        pv["xnwbf"] = None
        pv["szf"] = ev4[-1]
        pv["xhf"] = evy_all[-1]
        pv["bcf"] = tb3
        pv["cumf"] = tpk2
        pv["cmedf"] = cg[31]
        for bq in range(4):
            pv["pb%d" % bq] = cg[bq * 8 + 7]
        pv["pb4"] = evsp[-2]
        pv["pb5"] = evsp[-1]
        pv["pb6"] = pb_free[6]
        pv["pb7"] = pb_free[7]
    k.tok_ysp = tok_ysp_all
    k.dbg("ellt_w3", ellt[:, 0:SEG], t4)
    k.dbg("d0_w3c3", d0t[:], td0)
    k.dbg("tctm_w3c3", tctm[:], ttc[-1])
    k.dbg("Bm_w3c3", Bmt[:], tBm)
    k.dbg("sin_w3c3", sinb[:], ts1)
    k.dbg("cos_w3c3", cosb[:], ts2)
    k.dbg("B2_w3c3", AR1[:, 0:2048], tB2)
    k.dbg("Cm_w3c3", Cmt[:], tCm)
    k.dbg("xh_w3", xh_bf[:], evy_all[-1])
    k.dbg("yw_w3c3", ygp[:], ty1)
    k.dbg("Rw_fin", Rw[:], tfold)
    k.dbg("VLfin_w3", VLq[0][:], tpk1)
    k.dbg("vown_w3", vown[:], tvbf)
    k.dbg("cum_w3", cum[:], t_cum)
    k.dbg("apair_w3c3", apair[:], tap3)
    k.barrier()
    st.close()


QMAX = 32767.0

def _dequant_weight(k, psb, st, src_d, dst_bf, nrow, ncol, tag, dst_dram=None):
    T = k.T
    wld = k.sb("wld" + tag, (128, 2 * ncol), F32, st)
    bmx = k.sb("bmx" + tag, (128, ncol // 32), F32, st)
    wq = k.sb("wq" + tag, (128, ncol), I32, st)
    wdq = k.sb("wdq" + tag, (128, ncol), BF16, st)
    if dst_dram is not None:
        wstg = k.sb("wstg" + tag, (128, 256), BF16, st)
    nblk = ncol // 32
    for it in range(nrow // 128):
        b = it % 2
        wsl = wld[:, b * ncol:(b + 1) * ncol]
        k.dma("sync", wsl, src_d[it * 128:(it + 1) * 128, :], stream="wdq%d" % b)
        k.red(bmx[:], wsl.rearrange("p (a c) -> p a c", c=32), ALU.max, absv=True)
        k.ts(bmx[:], bmx[:], 1.0 / 7.0, 1e-12, ALU.mult, ALU.max)
        k.ex("vector", lambda e, s=bmx: e.reciprocal(s[:], s[:]),
             rd=[bmx[:]], wr=[bmx[:]])
        k.tt(wq[:].rearrange("p (a c) -> p a c", c=32),
             wsl.rearrange("p (a c) -> p a c", c=32),
             bmx[:].unsqueeze(2).broadcast_to((128, nblk, 32)), ALU.mult)
        k.ex("vector", lambda e, s=bmx: e.reciprocal(s[:], s[:]),
             rd=[bmx[:]], wr=[bmx[:]])
        k.tt(wdq[:].rearrange("p (a c) -> p a c", c=32),
             wq[:].rearrange("p (a c) -> p a c", c=32),
             bmx[:].unsqueeze(2).broadcast_to((128, nblk, 32)), ALU.mult)
        for fc in range(ncol // 128):
            bkp = fc % 2
            pbv = psb[bkp][:].bitcast(BF16)[0:128, 0:128]
            k.tr(pbv, wdq[:, fc * 128:(fc + 1) * 128], T["idnb"][:])
            if dst_dram is None:
                k.cp(dst_bf[:, fc * nrow + it * 128: fc * nrow + (it + 1) * 128],
                     pbv, eng="scalar")
            else:
                k.cp(wstg[:, (fc % 2) * 128:(fc % 2) * 128 + 128], pbv,
                     eng="scalar")
                k.dma("scalar",
                      dst_dram[fc * 128:(fc + 1) * 128,
                               it * 128:(it + 1) * 128],
                      wstg[:, (fc % 2) * 128:(fc % 2) * 128 + 128],
                      stream="wdqs%d" % (fc % 2))


def _finalize_max(k, psb, part_ap, ar_in, ar_out, scal_cols):
    T = k.T
    nvals = part_ap.shape[1]
    k.tr(psb[7][0:nvals, 0:128], part_ap)
    k.red(T["scr8"][0:nvals, 9:10], psb[7][0:nvals, 0:128], ALU.max)
    k.tr(psb[7][0:1, 128:128 + nvals], T["scr8"][0:nvals, 9:10])
    k.mset(T["scr8"][0:1, 0:8], 0.0)
    k.cp(T["scr8"][0:1, 0:nvals], psb[7][0:1, 128:128 + nvals])
    k.dma("gpsimd", ar_in[:], T["scr8"][0:1, 0:8], stream="armx")
    k.coll("AllReduce", ALU.max, RG_ALL, ar_in[:], ar_out[:])
    k.dma("gpsimd", T["scr8"][0:1, 0:8], ar_out[:], stream="armx")
    for j, (c_inv, c_scale) in enumerate(scal_cols):
        k.mm(psb[7][:, 128:129], T["ones_row"][:], T["scr8"][0:1, j:j + 1])
        k.ts(T["scal"][:, c_scale:c_scale + 1], psb[7][:, 128:129],
             1.0 / QMAX, 1e-12, ALU.mult, ALU.max)
        k.ex("vector", lambda e, ci=c_inv, cs=c_scale: e.reciprocal(
            T["scal"][:, ci:ci + 1], T["scal"][:, cs:cs + 1]),
            rd=[T["scal"][:, c_scale:c_scale + 1]],
            wr=[T["scal"][:, c_inv:c_inv + 1]])


def ph5(k, psb):
    D, T = k.D, k.T
    with ExitStack() as st:
        WgT = k.sb("WgT", (128, 8 * IFF), BF16, st)
        WuT = k.sb("WuT", (128, 8 * IFF), BF16, st)
        _dequant_weight(k, psb, st, D["Wg"], WgT, IFF, DM, "g")
        _dequant_weight(k, psb, st, D["Wu"], WuT, IFF, DM, "u")
        xnt = k.sb("xnt5", (128, 4096), F32, st)
        xq = k.sb("xq5", (128, 4096), I32, st)
        xqb = k.sb("xqb5", (128, 4096), BF16, st)
        stg = k.sb("stg5", (128, 4 * 512), F32, st)
        mxp = k.sb("mxp5", (128, 3), F32, st)
        k.mset(mxp[:], 0.0)
        for tt in range(4):
            k.dma("sync", xnt[:], D["xn_spill"].rearrange(
                "(a p) t -> p a t", p=128)[:, :, tt * SEG:(tt + 1) * SEG],
                stream="xnl")
            k.ts(xq[:], xnt[:], T["scal"][:, 2:3], None, ALU.mult, w=[k.tok_s1])
            k.ts(xqb[:], xq[:], T["scal"][:, 3:4], None, ALU.mult)
            for gu in range(2):
                WT = WgT if gu == 0 else WuT
                spd = D["gate_sp"] if gu == 0 else D["up_sp"]
                for oc in range(24):
                    bk = oc % 4
                    sb_ = (oc % 4)
                    for kc in range(8):
                        k.mm(psb[bk][0:128, 0:512],
                             WT[:, kc * IFF + oc * 128: kc * IFF + (oc + 1) * 128],
                             xqb[:, kc * SEG:(kc + 1) * SEG],
                             start=(kc == 0), stop=(kc == 7))
                    k.cp(stg[:, sb_ * 512:(sb_ + 1) * 512], psb[bk][0:128, 0:512],
                         eng="scalar")
                    k.dma("scalar", spd[oc * 128:(oc + 1) * 128,
                                        tt * SEG:(tt + 1) * SEG].rearrange(
                                            "p t -> p t"),
                          stg[:, sb_ * 512:(sb_ + 1) * 512],
                          stream="sp%d" % sb_)
                    k.red(mxp[:, 2:3], stg[:, sb_ * 512:(sb_ + 1) * 512],
                          ALU.max, absv=True)
                    k.tt(mxp[:, gu:gu + 1], mxp[:, gu:gu + 1], mxp[:, 2:3],
                         ALU.max)
        _finalize_max(k, psb, mxp[:, 0:2], D["ar23_in"], D["ar23_out"],
                      [(6, 7), (4, 5)])
    k.barrier()


def ph6(k, psb):
    D, T = k.D, k.T
    k.dbg("ysp_all", D["y_sp"][:], None, BF16)
    k.dbg("gate_sp0", D["gate_sp"][0:128, :], None)
    k.dbg("up_sp0", D["up_sp"][0:128, :], None)
    with ExitStack() as st:
        WoT = k.sb("WoT_bf", (128, 16 * DM), BF16, st)
        wst = k.sb("wst6", (128, 2 * DM), F32, st)
        ybf = k.sb("ybf6", (128, 16 * 512), BF16, st)
        sst = k.sb("sst6", (128, 512), F32, st)
        for kc in range(16):
            b = kc % 2
            k.dma("sync", wst[:, b * DM:(b + 1) * DM],
                  D["WoT"][kc * 128:(kc + 1) * 128, :], stream="wo%d" % b)
            k.cp(WoT[:, kc * DM:(kc + 1) * DM], wst[:, b * DM:(b + 1) * DM])
        for wv in range(4):
            k.dma("sync", ybf[:], D["y_sp"].rearrange(
                "(a p) t -> p a t", p=128)[:, :, wv * SEG:(wv + 1) * SEG],
                stream="yld")
            for oc in range(8):
                bk = oc % 4
                for kc in range(16):
                    k.mm(psb[bk][0:128, 0:512],
                         WoT[:, kc * DM + oc * 128: kc * DM + (oc + 1) * 128],
                         ybf[:, kc * 512:(kc + 1) * 512],
                         start=(kc == 0), stop=(kc == 15))
                k.cp(sst[:], psb[bk][0:128, 0:512], eng="scalar")
                k.dma("scalar", D["s_sp"][oc * 128:(oc + 1) * 128,
                                          wv * SEG:(wv + 1) * SEG], sst[:],
                      stream="ssp%d" % (oc % 2))
    k.barrier()


def _qdq_chain_tile(k, src_sb, inv_col, scale_col, out_sb):
    T = k.T
    k.ts(out_sb.bitcast(I32), src_sb, T["scal"][:, inv_col:inv_col + 1], None,
         ALU.mult)
    k.ts(out_sb, out_sb.bitcast(I32), T["scal"][:, scale_col:scale_col + 1],
         None, ALU.mult)


def ph7(k, psb):
    D, T = k.D, k.T
    with ExitStack() as st:
        g1 = k.sb("g7", (128, 2 * 2048), F32, st)
        gq = k.sb("gq7", (128, 2048), F32, st)
        sg7 = k.sb("sg7", (128, 2048), F32, st)
        tv = k.sb("tv7", (128, 2048), F32, st)
        mx = k.sb("mx7", (128, 2), F32, st)
        k.mset(mx[:], 0.0)
        for it in range(24):
            b = it % 2
            gsl = g1[:, b * 2048:(b + 1) * 2048]
            k.dma("sync", gsl, D["gate_sp"][it * 128:(it + 1) * 128, :],
                  stream="g7%d" % b)
            _qdq_chain_tile(k, gsl, 6, 7, gq[:])
            k.act(sg7[:], gq[:], AF.Sigmoid)
            k.ts(sg7[:].bitcast(I32), sg7[:], 65536.0, None, ALU.mult)
            k.ts(sg7[:], sg7[:].bitcast(I32), 65535.0, 1.0 / 65536.0,
                 ALU.min, ALU.mult)
            k.tt(tv[:], gq[:], sg7[:], ALU.mult)
            k.dma("scalar", D["t_sp"][it * 128:(it + 1) * 128, :], tv[:],
                  stream="t7")
            k.red(mx[:, 1:2], tv[:], ALU.max, absv=True)
            k.tt(mx[:, 0:1], mx[:, 0:1], mx[:, 1:2], ALU.max)
        _finalize_max(k, psb, mx[:, 0:1], D["ar4_in"], D["ar4_out"], [(8, 9)])
    k.barrier()


def ph8(k, psb):
    D, T = k.D, k.T
    with ExitStack() as st:
        t1 = k.sb("t8", (128, 2 * 2048), F32, st)
        u1 = k.sb("u8", (128, 2 * 2048), F32, st)
        tq = k.sb("tq8", (128, 2048), F32, st)
        uq = k.sb("uq8", (128, 2048), F32, st)
        uv = k.sb("uv8", (128, 2048), F32, st)
        mx = k.sb("mx8", (128, 2), F32, st)
        k.mset(mx[:], 0.0)
        for it in range(24):
            b = it % 2
            tsl = t1[:, b * 2048:(b + 1) * 2048]
            usl = u1[:, b * 2048:(b + 1) * 2048]
            k.dma("sync", tsl, D["t_sp"][it * 128:(it + 1) * 128, :],
                  stream="t8%d" % b)
            k.dma("sync", usl, D["up_sp"][it * 128:(it + 1) * 128, :],
                  stream="u8%d" % b)
            _qdq_chain_tile(k, tsl, 8, 9, tq[:])
            _qdq_chain_tile(k, usl, 4, 5, uq[:])
            k.tt(uv[:], tq[:], uq[:], ALU.mult)
            k.dma("scalar", D["u_sp"][it * 128:(it + 1) * 128, :], uv[:],
                  stream="u8s")
            k.red(mx[:, 1:2], uv[:], ALU.max, absv=True)
            k.tt(mx[:, 0:1], mx[:, 0:1], mx[:, 1:2], ALU.max)
        _finalize_max(k, psb, mx[:, 0:1], D["ar5_in"], D["ar5_out"], [(10, 11)])
    k.mset(k.T["scal"][:, 12:16], 0.0)
    k.mset(k.T["scal"][:, 0:1], 0.0)
    k.dbg("scal_all", k.T["scal"][:, 0:16], None)
    k.dbg("t_sp0", D["t_sp"][0:128, :], None)
    k.dbg("u_sp0", D["u_sp"][0:128, :], None)
    k.barrier()


def ph9(k, psb):
    D, T = k.D, k.T
    k.dbg("s_sp_all", D["s_sp"][:], None)
    with ExitStack() as st:
        _dequant_weight(k, psb, st, D["Wd"], None, DM, IFF, "d",
                        dst_dram=D["WdTH"])
    with ExitStack() as st:
        u9 = k.sb("u9", (128, 2 * 512), F32, st)
        i9 = k.sb("i9", (128, 512), I32, st)
        ob9 = k.sb("ob9", (128, 24 * 512), BF16, st)
        wd9 = k.sb("wd9", (128, 24 * DM), BF16, st)
        gfn = k.sb("gfn9", (128, 512), F32, st)
        sg9 = k.sb("sg9", (128, 8 * 512), F32, st)
        s9 = k.sb("s9", (128, 8 * 512), F32, st)
        xre = k.sb("xre9", (128, 4 * DM), F32, st)
        ost = k.sb("ost9", (128, DM), F32, st)
        for tt in range(4):
            k.dma("sync", xre[:].rearrange("p (a c) -> p a c", a=4),
                  D["x"].rearrange("(a p) c -> p a c", p=128)
                  [:, tt * 4:(tt + 1) * 4, :], stream="x9l")
            k.dma("sync", s9[:].rearrange("p (a c) -> p a c", a=8),
                  D["s_sp"].rearrange("(a p) t -> p a t", p=128)
                  [:, :, tt * SEG:(tt + 1) * SEG], stream="s9l")
            for kc in range(24):
                b = kc % 2
                usl = u9[:, b * 512:(b + 1) * 512]
                k.dma("sync", usl,
                      D["u_sp"][kc * 128:(kc + 1) * 128, tt * SEG:(tt + 1) * SEG],
                      stream="u9%d" % b)
                k.ts(i9[:], usl, T["scal"][:, 10:11], None, ALU.mult)
                k.ts(ob9[:, kc * 512:(kc + 1) * 512], i9[:],
                     T["scal"][:, 11:12], None, ALU.mult)
            for oc in range(8):
                bk = oc % 4
                for kc in range(24):
                    if oc == 0 and tt == 0:
                        k.dma("sync", wd9[:, kc * DM:(kc + 1) * DM],
                              D["WdTH"].rearrange("(a p) c -> p a c", p=128)
                              [:, kc, :], stream="wd9%d" % (kc % 2))
                    k.mm(psb[bk][0:128, 0:512],
                         wd9[:, kc * DM + oc * 128: kc * DM + (oc + 1) * 128],
                         ob9[:, kc * 512:(kc + 1) * 512],
                         start=(kc == 0), stop=(kc == 23))
                k.act(gfn[:], psb[bk][0:128, 0:512], AF.Silu)
                k.tt(sg9[:, oc * 512:(oc + 1) * 512],
                     s9[:, oc * 512:(oc + 1) * 512], gfn[:], ALU.mult)
            for tb in range(4):
                for oc in range(8):
                    bkp = 4 + (oc % 2)
                    k.tr(psb[bkp][0:128, 0:128],
                         sg9[:, oc * 512 + tb * 128: oc * 512 + (tb + 1) * 128])
                    k.ex("vector", lambda e, tb=tb, oc=oc, bkp=bkp:
                         e.scalar_tensor_tensor(
                             ost[:, oc * 128:(oc + 1) * 128],
                             xre[:, tb * DM + oc * 128: tb * DM + (oc + 1) * 128],
                             0.0, psb[bkp][0:128, 0:128], ALU.add, ALU.add),
                         rd=[xre[:, 0:1], psb[bkp][0:1, 0:1]],
                         wr=[ost[:, 0:1]])
                k.dma("scalar",
                      D["out"][tt * SEG + tb * 128: tt * SEG + (tb + 1) * 128, :],
                      ost[:], stream="outw")
    k.barrier()


import ml_dtypes

def host_shards(inp):
    f32 = np.float32
    x_full = np.ascontiguousarray(inp["L0_attn_u"], dtype=f32)
    Wp = inp["L0_attn_in_proj_w"].astype(f32)
    WpT = np.ascontiguousarray(Wp.T)
    WoT = np.ascontiguousarray(inp["L0_attn_out_proj_w"].astype(f32).T)
    Wg = np.ascontiguousarray(inp["L0_ffn_w_gate"], dtype=f32)
    Wu = np.ascontiguousarray(inp["L0_ffn_w_up"], dtype=f32)
    Wd = np.ascontiguousarray(inp["L0_ffn_w_down"], dtype=f32)

    smalls = np.zeros((8, 128), f32)
    smalls[0, 0:32] = inp["L0_attn_D"]
    smalls[1, 0:32] = inp["L0_attn_dt_bias"]
    smalls[2, 0:32] = inp["L0_attn_A_log"]
    smalls[3, 0:64] = inp["L0_attn_B_norm_w"]
    smalls[4, 0:64] = inp["L0_attn_C_norm_w"]

    bias_bc = np.zeros((128, 2 * H * N), f32)
    bias_bc[:, 0:H * N] = inp["L0_attn_B_bias"].reshape(1, -1)
    bias_bc[:, H * N:] = inp["L0_attn_C_bias"].reshape(1, -1)
    bias_bc = bias_bc.astype(ml_dtypes.bfloat16)
    wn_bc = np.broadcast_to(inp["L0_attn_norm_w"].astype(f32)[None, :],
                            (128, DI)).astype(ml_dtypes.bfloat16).copy()
    wbc = np.zeros((128, 2 * N), f32)
    wbc[:, 0:N] = inp["L0_attn_B_norm_w"]
    wbc[:, N:] = inp["L0_attn_C_norm_w"]

    idn_f = np.eye(128, dtype=f32)
    idn_b = np.eye(128, dtype=f32).astype(ml_dtypes.bfloat16)
    s_ = np.arange(128)[:, None]; i_ = np.arange(128)[None, :]
    tribig = np.where(i_ >= s_, BIG, 0.0).astype(f32).astype(ml_dtypes.bfloat16)

    maps = []
    for c in range(8):
        seq, lane = c // 4, c % 4
        rows = []
        for w in range(NW):
            g = 4 * w + lane
            rows.append(x_full[seq, g * SEG:(g + 1) * SEG, :])
        x_core = np.concatenate(rows, axis=0)
        xnext = np.zeros((NW, DM), f32)
        for w in range(NW):
            g = 4 * w + lane
            if g + 1 < 16:
                xnext[w] = x_full[seq, (g + 1) * SEG, :]
        masks = np.zeros((128, 16), f32)
        for j in range(4):
            masks[:, j] = 1.0 if j < lane else 0.0
            masks[:, 8 + j] = 1.0 if j == lane else 0.0
        for w in range(NW):
            g = 4 * w + lane
            masks[:, 4 + w] = 1.0 if (g + 1 < 16) else 0.0
        rows3 = np.zeros((1, 384), f32)
        rows3[0, 0:128] = 1.0
        rows3[0, 128:192] = 1.0
        rows3[0, 320:384] = 1.0
        e0mat = np.zeros((128, 128), f32); e0mat[0, :] = 1.0
        kk = np.arange(128)
        mska = (kk[:, None] < kk[None, :]).astype(f32).astype(ml_dtypes.bfloat16)
        mskb = (kk[:, None] >= kk[None, :]).astype(f32) * (-1.0e30)
        mskb4 = np.tile(mskb, (1, 4)).astype(ml_dtypes.bfloat16)
        maps.append(dict(
            x=np.ascontiguousarray(x_core), xnext=xnext, rows3=rows3,
            e0mat=e0mat, mska=mska, mskb4=mskb4,
            WpT=WpT, WoT=WoT, Wg=Wg, Wu=Wu, Wd=Wd,
            smalls=smalls, bias_bc=bias_bc, wn_bc=wn_bc, wbc_bc=wbc,
            idn_f=idn_f, idn_b=idn_b, tribig=tribig, masks=masks,
        ))
    return maps


def kernel(**inputs):
    from concourse.bass_utils import run_bass_kernel_spmd
    nc, _ = build(9, ())
    maps = host_shards(inputs)
    r = run_bass_kernel_spmd(nc, maps, core_ids=list(range(8)))
    res = r.results
    out = np.zeros((2, 8192, 1024), np.float32)
    for c in range(8):
        seq, lane = c // 4, c % 4
        o = np.asarray(res[c]["out"], dtype=np.float32)
        for w in range(NW):
            g = 4 * w + lane
            out[seq, g * SEG:(g + 1) * SEG, :] = o[w * SEG:(w + 1) * SEG, :]
    return out


# version 02 (final)
# speedup vs baseline: 1.0103x; 1.0103x over previous
import numpy as np
from contextlib import ExitStack
import concourse.bass as bass
import concourse.mybir as mybir
from concourse.alu_op_type import AluOpType as ALU

F32 = mybir.dt.float32
BF16 = mybir.dt.bfloat16
I32 = mybir.dt.int32
AF = mybir.ActivationFunctionType
AX = mybir.AxisListType

H, P, N, DM, DI, DPROJ = 32, 64, 64, 1024, 2048, 5312
OFF_Z, OFF_XIN, OFF_B, OFF_C = 0, 2048, 4096, 4160
OFF_DT, OFF_THETA, OFF_LAM = 4224, 4256, 5280
TPC, NW, SEG, Q = 2048, 4, 512, 128
NTB = 16
IFF = 3072
BIG = 3.0e38
EPS = 1e-5
INV2PI = float(1.0 / (2.0 * np.pi))
TWOPI = float(2.0 * np.pi)
HALFPI = float(np.pi / 2.0)
RG_SEQ = [[0, 1, 2, 3], [4, 5, 6, 7]]
RG_ALL = [[0, 1, 2, 3, 4, 5, 6, 7]]


class K:
    ENGINES = ("sync", "tensor", "vector", "scalar", "gpsimd")

    def __init__(self, stage=99, dbg_names=()):
        self.stage = stage
        self.dbg_req = set(dbg_names)
        self.dbg_out = []
        self.nc = bass.Bass("TRN2", target_bir_lowering=False,
                            detect_race_conditions=False)
        self.es = ExitStack()
        self.D = {}
        self.T = {}
        self.ops = {e: [] for e in self.ENGINES}
        self.semh = {}
        self.semn = {}
        self.wm = {e: {} for e in self.ENGINES}
        self.recent = {e: [] for e in self.ENGINES}
        self.tw = {}
        self.trd = {}
        self._dbg_ctr = 0

    def _sem(self, name):
        if name not in self.semh:
            self.semh[name] = self.es.enter_context(self.nc.semaphore(name))
            self.semn[name] = 0
        return self.semh[name]

    @staticmethod
    def _apnames(aps):
        out = set()
        for a in aps:
            if a is None:
                continue
            try:
                out.add(a.tensor.name)
            except Exception:
                try:
                    out.add(a.name)
                except Exception:
                    pass
        return out

    def ex(self, eng, emitfn, waits=(), pub=None, pubk=None, rd=(), wr=(),
           ckp=True):
        if ckp and eng != "sync":
            for a in list(rd) + list(wr):
                if a is None or isinstance(a, (int, float)):
                    continue
                try:
                    sp = a.offset // a.ap[0][0]
                except Exception:
                    continue
                if sp % 32 != 0:
                    import traceback
                    st_ = traceback.extract_stack()
                    print(f"PARTSTART-BAD sp={sp} shape={a.shape} at " +
                          " <- ".join(f"{f.filename.split('/')[-1]}:{f.lineno}"
                                      for f in st_[-5:-1]))
        wm = self.wm[eng]
        waits = [t for t in waits if t is not None]
        rdn = self._apnames(rd)
        wrn = self._apnames(wr)
        for n in rdn:
            t = self.tw.get(n)
            if t is not None:
                waits.append(t)
        for n in wrn:
            t = self.tw.get(n)
            if t is not None:
                waits.append(t)
            for s, v in self.trd.get(n, {}).items():
                waits.append((s, v))
        if eng in ("vector", "scalar", "gpsimd"):
            names = rdn | wrn
            for wn, tok in self.recent[eng]:
                if wn & names:
                    waits.append(tok)
        needed = {}
        for t in waits:
            if t is None:
                continue
            s, v = t
            if v > wm.get(s, 0) and v > needed.get(s, 0):
                needed[s] = v
        for s, v in needed.items():
            h = self._sem(s)
            self.ops[eng].append(lambda e, h=h, v=v: e.wait_ge(h, v))
            wm[s] = v
        if pub is None:
            pub = {"sync": "d_s", "scalar": "sE", "vector": "vE",
                   "tensor": "tE", "gpsimd": "gE"}[eng]
        if pubk is None:
            pubk = 16 if pub.startswith("d_") else 1
        h = self._sem(pub)
        self.semn[pub] += pubk
        val = self.semn[pub]
        def f(e, emitfn=emitfn, h=h, k=pubk):
            emitfn(e).then_inc(h, k)
        self.ops[eng].append(f)
        tok = (pub, val)
        if eng in ("vector", "scalar", "gpsimd") and wrn:
            self.recent[eng].append((wrn, tok))
            if len(self.recent[eng]) > 12:
                self.recent[eng].pop(0)
        for n in wrn:
            self.tw[n] = tok
            self.trd[n] = {}
        for n in rdn:
            d = self.trd.setdefault(n, {})
            if tok[1] > d.get(tok[0], 0):
                d[tok[0]] = tok[1]
        return tok

    def barrier(self):
        toks = {}
        for e in self.ENGINES:
            s = "bar_" + e
            h = self._sem(s)
            self.semn[s] += 1
            v = self.semn[s]
            if e == "sync":
                self.ops[e].append(lambda eng, h=h: eng.nop().then_inc(h, 1))
            elif e in ("tensor", "scalar"):
                self.ops[e].append(lambda eng, h=h: eng.drain().then_inc(h, 1))
            else:
                self.ops[e].append(
                    lambda eng, h=h: eng.engine_nop().then_inc(h, 1))
            toks[e] = (s, v)
        dma_now = [(s, self.semn[s]) for s in self.semn
                   if s.startswith("d_") and self.semn[s] > 0]
        for e in self.ENGINES:
            need = [toks[e2] for e2 in self.ENGINES if e2 != e] + dma_now
            for s, v in need:
                if v > self.wm[e].get(s, 0):
                    h = self._sem(s)
                    self.ops[e].append(lambda eng, h=h, v=v: eng.wait_ge(h, v))
                    self.wm[e][s] = v

    def mm(self, out, lhsT, rhs, start=True, stop=True, w=(), tp=None):
        return self.ex("tensor", lambda e: e.matmul(
            out, lhsT, rhs, start=start, stop=stop, tile_position=tp), waits=w,
            rd=[lhsT, rhs], wr=[out])

    def tr(self, out, in_, idn=None, w=()):
        if idn is None:
            kp = in_.shape[0]
            b0 = in_.base_partition() if callable(in_.base_partition) else in_.base_partition
            idn = self.T["consts"][b0:b0 + kp, b0:b0 + kp]
        return self.ex("tensor", lambda e: e.transpose(out, in_, idn), waits=w,
                       rd=[in_, idn], wr=[out])

    @staticmethod
    def canon_base(ap):
        try:
            b = ap.base_partition() if callable(ap.base_partition) else ap.base_partition
            return b
        except AssertionError as ex:
            import re
            m = re.search(r"got (\d+)", str(ex))
            if m:
                n = int(m.group(1))
                return 64 if n >= 64 else (32 if n >= 32 else 0)
            raise

    @staticmethod
    def start_part(ap):
        try:
            return ap.offset // ap.ap[0][0]
        except Exception:
            return None

    def _ckbase(self, tag, a, b):
        try:
            ba, bb = self.start_part(a), self.start_part(b)
            if ba is None or bb is None:
                ba, bb = self.canon_base(a), self.canon_base(b)
            if ba != bb:
                import traceback
                st_ = traceback.extract_stack()
                print(f"{tag}-BASEMISMATCH a={ba} b={bb} shapes={a.shape}/{b.shape} at " +
                      " <- ".join(f"{f.filename.split('/')[-1]}:{f.lineno}" for f in st_[-5:-2]))
        except Exception as ex:
            print(f"{tag}-BASECHECKFAIL {type(ex).__name__} {ex}")

    def tt(self, out, a, b, op, w=(), eng="vector"):
        self._ckbase("TT", a, b)
        return self.ex(eng, lambda e: e.tensor_tensor(out, a, b, op), waits=w,
                       rd=[a, b], wr=[out])

    def ts(self, out, a, s1, s2=None, op0=ALU.mult, op1=None, w=(), eng="vector"):
        rds = [a] + [x for x in (s1, s2) if not isinstance(x, (int, float, type(None)))]
        if op1 is None:
            return self.ex(eng, lambda e: e.tensor_scalar(out, a, s1, None, op0),
                           waits=w, rd=rds, wr=[out])
        return self.ex(eng, lambda e: e.tensor_scalar(out, a, s1, s2, op0, op1),
                       waits=w, rd=rds, wr=[out])

    def stt(self, out, in0, scalar, in1, op0, op1, w=()):
        self._ckbase("STT", in0, in1)
        rds = [in0, in1] + ([scalar] if not isinstance(scalar, (int, float)) else [])
        return self.ex("vector", lambda e: e.scalar_tensor_tensor(
            out, in0, scalar, in1, op0, op1), waits=w, rd=rds, wr=[out])

    def act(self, out, in_, f, bias=None, scale=None, w=()):
        kw = {}
        if bias is not None:
            kw["bias"] = bias
        if scale is not None:
            kw["scale"] = scale
        rds = [in_] + [v for kk, v in kw.items() if not isinstance(v, (int, float))]
        return self.ex("scalar", lambda e: e.activation(out, in_, f, **kw), waits=w,
                       rd=rds, wr=[out])

    def red(self, out, in_, op=ALU.add, absv=None, w=(), eng="vector"):
        return self.ex(eng, lambda e: e.tensor_reduce(
            out, in_, AX.X, op, apply_absolute_value=absv), waits=w,
            rd=[in_], wr=[out])

    def scan(self, out, d0, d1, initial, op0, op1, w=()):
        self._ckbase("SCAN", d0, d1)
        rds = [d0, d1] + ([initial] if not isinstance(initial, (int, float)) else [])
        return self.ex("vector", lambda e: e.tensor_tensor_scan(
            out, d0, d1, initial, op0, op1), waits=w, rd=rds, wr=[out])

    def cp(self, out, in_, w=(), eng="vector"):
        if eng == "scalar":
            return self.ex("scalar", lambda e: e.copy(out, in_), waits=w,
                           rd=[in_], wr=[out])
        return self.ex(eng, lambda e: e.tensor_copy(out, in_), waits=w,
                       rd=[in_], wr=[out])

    def mset(self, ap, v, w=(), eng="vector"):
        return self.ex(eng, lambda e: e.memset(ap, v), waits=w, wr=[ap])

    def dma(self, q, out, in_, w=(), stream=None):
        base = {"sync": "d_s", "scalar": "d_sc", "gpsimd": "d_g",
                "vector": "d_v"}[q]
        pub = base if stream is None else base + "_" + stream
        return self.ex(q, lambda e: e.dma_start(out=out, in_=in_), waits=w, pub=pub,
                       rd=[in_], wr=[out], ckp=False)

    def coll(self, kind, op, rg, in_ap, out_ap, w=()):
        return self.ex("gpsimd", lambda e: e.collective_compute(
            kind, op, replica_groups=rg, ins=[in_ap], outs=[out_ap]), waits=w,
            pub="gE", rd=[in_ap], wr=[out_ap], ckp=False)

    def din(self, name, shape, dtype=F32):
        self.D[name] = self.nc.dram_tensor(name, list(shape), dtype,
                                           kind="ExternalInput").ap()
    def dout(self, name, shape, dtype=F32):
        self.D[name] = self.nc.dram_tensor(name, list(shape), dtype,
                                           kind="ExternalOutput").ap()
    def dint(self, name, shape, dtype=F32):
        self.D[name] = self.nc.dram_tensor(name, list(shape), dtype,
                                           kind="Internal").ap()
    def sb(self, name, shape, dtype=F32, st=None):
        self.T[name] = (st or self.es).enter_context(
            self.nc.sbuf_tensor(name, list(shape), dtype))
        return self.T[name]

    def dbg(self, name, ap, tok, dtype=None):
        if name not in self.dbg_req:
            return
        nm = "dbg_" + name
        self.dout(nm, tuple(ap.shape), dtype or F32)
        self.dbg_out.append((nm, tuple(ap.shape)))
        self.ex("gpsimd", lambda e, d=self.D[nm], ap=ap: e.dma_start(out=d[:], in_=ap),
                waits=[tok], pub="d_g", rd=[ap], wr=[self.D[nm]])

    def finish(self):
        self.barrier()
        with self.nc.Block() as block:
            for e in self.ENGINES:
                lst = self.ops[e]
                def make(mylist):
                    def body(eng):
                        for f in mylist:
                            f(eng)
                    return body
                getattr(block, e)(make(lst))
        self.es.close()
        return self.nc


def build(stage=99, dbg_names=()):
    k = K(stage, dbg_names)
    D, T = k.D, k.T

    k.din("x", (TPC, DM)); k.din("xnext", (NW, DM))
    k.din("WpT", (DM, DPROJ)); k.din("WoT", (DI, DM))
    k.din("Wg", (IFF, DM)); k.din("Wu", (IFF, DM)); k.din("Wd", (DM, IFF))
    k.din("smalls", (8, 128))
    k.din("bias_bc", (128, 2 * H * N), BF16)
    k.din("wn_bc", (128, DI), BF16)
    k.din("wbc_bc", (128, 2 * N))
    k.din("idn_f", (128, 128)); k.din("idn_b", (128, 128), BF16)
    k.din("tribig", (128, 128), BF16)
    k.din("masks", (128, 16))
    k.din("rows3", (1, 384))
    k.din("e0mat", (128, 128))
    k.din("mska", (128, 128), BF16)
    k.din("mskb4", (128, 512), BF16)
    k.dout("out", (TPC, DM))
    k.dint("xn_spill", (DM, TPC))
    k.dint("y_sp", (DI, TPC), BF16)
    k.dint("gate_sp", (IFF, TPC)); k.dint("up_sp", (IFF, TPC))
    k.dint("t_sp", (IFF, TPC)); k.dint("u_sp", (IFF, TPC))
    k.dint("s_sp", (DM, TPC)); k.dint("WdTH", (IFF, DM), BF16)
    k.dint("ag1_in", (128, 36)); k.dint("ag1_out", (512, 36))
    k.dint("armax_in", (1, 8)); k.dint("armax_out", (1, 8))
    for w in range(NW):
        k.dint("ag2_in%d" % w, (128, 1040))
        k.dint("ag2_out%d" % w, (512, 1040))
    k.dint("ar23_in", (1, 8)); k.dint("ar23_out", (1, 8))
    k.dint("ar4_in", (1, 8)); k.dint("ar4_out", (1, 8))
    k.dint("ar5_in", (1, 8)); k.dint("ar5_out", (1, 8))

    k.sb("consts", (128, 128))
    k.sb("idnb", (128, 128), BF16)
    k.sb("trib", (128, 128), BF16)
    k.sb("biasbc", (128, 2 * H * N), BF16)
    k.sb("wnbc", (128, DI), BF16)
    k.sb("wbcbc", (128, 2 * N))
    k.sb("smalls_s", (8, 128))
    k.sb("masks_s", (128, 16))
    k.sb("rows3_s", (1, 384))
    k.sb("e0mat_s", (128, 128))
    k.sb("mska_s", (128, 128), BF16)
    k.sb("mskb4_s", (128, 512), BF16)
    k.sb("small2", (32, 8))
    k.sb("Drow", (128, 32))
    k.sb("ones_row", (1, 128))
    k.sb("scal", (128, 16))
    k.sb("cvec", (128, 4))
    k.sb("csum", (128, 32))
    k.sb("maxp", (128, 4))
    k.sb("klb", (32, 8))
    k.sb("carryT", (128, 32))
    k.sb("scr8", (128, 16))
    psb = [k.es.enter_context(k.nc.psum_tensor("pb%d" % i, [128, 512], F32))
           for i in range(8)]
    k.psb = psb

    ph0(k, psb)
    if k.stage >= 1:
        ph1(k, psb)
    if k.stage >= 2:
        ph2(k, psb)
    if k.stage >= 3:
        ph3(k, psb)
    if k.stage >= 4:
        ph4(k, psb)
    if k.stage >= 5:
        ph5(k, psb)
    if k.stage >= 6:
        ph6(k, psb)
    if k.stage >= 7:
        ph7(k, psb)
    if k.stage >= 8:
        ph8(k, psb)
    if k.stage >= 9:
        ph9(k, psb)
    nc = k.finish()
    return nc, k.dbg_out


def ph0(k, psb):
    D, T = k.D, k.T
    k.din("Wpbf", (DM, DPROJ), BF16)
    k.mset(T["cvec"][:, 0:1], EPS)
    k.mset(T["cvec"][:, 1:2], 1.0)
    tok_c = None
    for (dn, tn) in [("idn_f", "consts"), ("idn_b", "idnb"), ("tribig", "trib"),
                     ("bias_bc", "biasbc"), ("wn_bc", "wnbc"), ("wbc_bc", "wbcbc"),
                     ("smalls", "smalls_s"), ("masks", "masks_s"), ("rows3", "rows3_s"), ("e0mat", "e0mat_s"), ("mska", "mska_s"), ("mskb4", "mskb4_s")]:
        tok_c = k.dma("sync", T[tn][:], D[dn][:])
    k.tok_consts = tok_c
    with k.nc.sbuf_tensor("wstg", [128, 2 * DPROJ], F32) as wstg, \
            k.nc.sbuf_tensor("wbstg", [128, 2 * DPROJ], BF16) as wbstg:
        k.tok_wp = None
        k.tok_wp0 = None
        t_tr = k.tr(psb[0][0:128, 0:8], T["smalls_s"][:], w=[tok_c])
        k.ts(T["small2"][:, 0:1], psb[0][0:32, 1:2], -1.0, w=[t_tr])
        k.act(T["small2"][:, 1:2], psb[0][0:32, 2:3], AF.Exp, w=[t_tr])
        t_s2 = k.cp(T["small2"][:, 2:3], psb[0][0:32, 0:1], w=[t_tr])
        t_on = k.mset(T["ones_row"][:], 1.0)
        t_dm = k.mm(psb[1][:, 0:32], T["ones_row"][:], T["smalls_s"][0:1, 0:32],
                    w=[t_on, tok_c, t_tr])
        t_dr = k.cp(T["Drow"][:], psb[1][:, 0:32], w=[t_dm])
        k.tok_small2 = t_s2
        k.dbg("small2", T["small2"][:, 0:3], t_s2)
        k.dbg("Drow", T["Drow"][:], t_dr)
        k.barrier()


def ph1(k, psb):
    D, T = k.D, k.T
    with ExitStack() as st:
        xbuf = k.sb("xbuf", (128, 2 * DM), F32, st)
        sq = k.sb("sqt", (128, DM), F32, st)
        stat = k.sb("stat", (128, 16), F32, st)
        xn_tm = k.sb("xn_tm1", (128, DM), F32, st)
        ev = k.sb("ev_fp", (128, 2 * DM), F32, st)
        k.mset(T["maxp"][:, 0:1], 0.0)
        t0 = k.mset(T["csum"][:], 0.0)
        free_x = [None, None]
        ev_free = [None, None]
        last_tr = {}
        stat_free = None
        xn_free = None
        for tb in range(NTB):
            b = tb % 2
            xt = xbuf[:, b * DM:(b + 1) * DM]
            tl = k.dma("sync", xt, D["x"][tb * 128:(tb + 1) * 128, :], w=[free_x[b]],
                       stream="x%d" % b)
            k.red(stat[:, 0:1], xt, ALU.add, w=[tl, stat_free])
            k.tt(sq[:], xt, xt, ALU.mult, w=[tl])
            k.red(stat[:, 1:2], sq[:], ALU.add)
            k.ts(stat[:, 2:3], stat[:, 0:1], 1.0 / DM)
            k.tt(stat[:, 3:4], stat[:, 2:3], stat[:, 2:3], ALU.mult)
            k.ts(stat[:, 5:6], stat[:, 1:2], 1.0 / DM)
            tv = k.tt(stat[:, 6:7], stat[:, 5:6], stat[:, 3:4], ALU.subtract)
            k.act(stat[:, 7:8], stat[:, 6:7], AF.Ln, bias=T["cvec"][:, 0:1], w=[tv])
            tr_ = k.act(stat[:, 8:9], stat[:, 7:8], AF.Exp, scale=-0.5)
            stat_free = tr_
            k.ts(stat[:, 9:10], stat[:, 2:3], -1.0)
            tx = k.ts(xn_tm[:], xt, stat[:, 9:10], stat[:, 8:9], ALU.add, ALU.mult,
                      w=[tr_, xn_free])
            free_x[b] = tx
            ttr = []
            for fc in range(8):
                ttr.append(k.tr(psb[fc][:, 0:128], xn_tm[:, fc * 128:(fc + 1) * 128],
                                w=[tx, last_tr.get(fc)]))
            xn_free = ttr[-1]
            eb = tb % 2
            evs = []
            for fc in range(8):
                ta = k.cp(ev[:, eb * DM + fc * 128: eb * DM + (fc + 1) * 128],
                          psb[fc][:, 0:128],
                          w=[ttr[fc]] + (ev_free[eb] or []), eng="scalar")
                evs.append(ta)
                last_tr[fc] = ta
            tsp = k.dma("scalar",
                        D["xn_spill"].rearrange("(a p) t -> p a t", p=128)
                        [:, :, tb * 128:(tb + 1) * 128],
                        ev[:, eb * DM:(eb + 1) * DM].rearrange("p (a c) -> p a c", a=8),
                        w=[evs[-1]])
            w_ = tb // 4
            t_c1 = None
            for fc in range(8):
                t_c1 = k.red(T["scr8"][:, fc:fc + 1],
                             ev[:, eb * DM + fc * 128: eb * DM + (fc + 1) * 128],
                             ALU.add, w=[evs[fc], t0])
            t_c2 = k.tt(T["csum"][:, w_ * 8:(w_ + 1) * 8],
                        T["csum"][:, w_ * 8:(w_ + 1) * 8], T["scr8"][:, 0:8], ALU.add,
                        w=[t_c1])
            k.red(T["scr8"][:, 8:9], ev[:, eb * DM:(eb + 1) * DM], ALU.max, absv=True)
            t_m = k.tt(T["maxp"][:, 0:1], T["maxp"][:, 0:1], T["scr8"][:, 8:9], ALU.max)
            ev_free[eb] = [tsp, t_c1, t_m]
        k.dbg("stat", stat[:], stat_free)
        k.dbg("csum", T["csum"][:], t_c2)
        k.dbg("maxp", T["maxp"][:, 0:1], t_m)
        k.dbg("xn_last", xn_tm[:], t_m)
    k.barrier()


def ph2(k, psb):
    D, T = k.D, k.T
    with ExitStack() as st:
        xb = k.sb("xnb", (NW, DM), F32, st)
        stat = k.sb("stat2", (NW, 16), F32, st)
        sq = k.sb("sq2", (NW, DM), F32, st)
        xnn = k.sb("xnn", (NW, DM), F32, st)
        xnf = k.sb("xnf", (128, 8 * NW), F32, st)
        dl = k.sb("dlnext", (NW, 64), F32, st)
        dlT = k.sb("dlT", (64, NW), F32, st)
        tl = k.dma("sync", xb[:], D["xnext"][:])
        k.red(stat[:, 0:1], xb[:], ALU.add, w=[tl])
        k.tt(sq[:], xb[:], xb[:], ALU.mult, w=[tl])
        k.red(stat[:, 1:2], sq[:], ALU.add)
        k.ts(stat[:, 2:3], stat[:, 0:1], 1.0 / DM)
        k.tt(stat[:, 3:4], stat[:, 2:3], stat[:, 2:3], ALU.mult)
        k.ts(stat[:, 5:6], stat[:, 1:2], 1.0 / DM)
        tv = k.tt(stat[:, 6:7], stat[:, 5:6], stat[:, 3:4], ALU.subtract)
        k.act(stat[:, 7:8], stat[:, 6:7], AF.Ln, bias=T["cvec"][0:NW, 0:1], w=[tv])
        tr_ = k.act(stat[:, 8:9], stat[:, 7:8], AF.Exp, scale=-0.5)
        k.ts(stat[:, 9:10], stat[:, 2:3], -1.0)
        tx = k.ts(xnn[:], xb[:], stat[:, 9:10], stat[:, 8:9], ALU.add, ALU.mult,
                  w=[tr_])
        evs = []
        for fc in range(8):
            tt = k.tr(psb[fc][0:128, 0:NW], xnn[:, fc * 128:(fc + 1) * 128], w=[tx])
            evs.append(k.cp(xnf[:, fc * NW:(fc + 1) * NW], psb[fc][:, 0:NW], w=[tt],
                            eng="scalar"))
        wdl = k.sb("wdl", (128, 2 * 64), F32, st)
        mmtoks = []
        fwd = [None, None]
        for kc in range(8):
            b = kc % 2
            td1 = k.dma("sync", wdl[:, b * 64: b * 64 + 32],
                        D["WpT"][kc * 128:(kc + 1) * 128, OFF_DT:OFF_DT + 32],
                        w=[fwd[b]], stream="wdl%d" % b)
            td2 = k.dma("sync", wdl[:, b * 64 + 32: b * 64 + 64],
                        D["WpT"][kc * 128:(kc + 1) * 128, OFF_LAM:OFF_LAM + 32],
                        w=[fwd[b]], stream="wdl%d" % b)
            lhsT = xnf[:, kc * NW:(kc + 1) * NW]
            t2 = k.mm(psb[6][0:NW, 0:64], lhsT, wdl[:, b * 64:(b + 1) * 64],
                      start=(kc == 0), stop=(kc == 7),
                      w=evs + [td1, td2] if kc == 0 else [evs[kc], td1, td2])
            fwd[b] = t2
            mmtoks.append(t2)
        tev = k.cp(dl[:], psb[6][0:NW, 0:64], w=[mmtoks[-1]])
        ttr = k.tr(psb[7][0:64, 0:NW], dl[:], w=[tev])
        tevT = k.cp(dlT[:], psb[7][0:64, 0:NW], w=[ttr])
        sg = k.act(T["klb"][:, 0:NW], dlT[0:32, 0:NW], AF.Sigmoid,
                   bias=T["small2"][:, 0:1], scale=-1.0, w=[tevT, k.tok_small2])
        t1 = k.act(T["klb"][:, 4:4 + NW], T["klb"][:, 0:NW], AF.Ln)
        t2 = k.act(T["klb"][:, 0:NW], dlT[32:64, 0:NW], AF.Sigmoid)
        t3 = k.tt(T["klb"][:, 0:NW], T["klb"][:, 0:NW], T["klb"][:, 4:4 + NW], ALU.mult,
                  w=[t1, t2])
        t4 = k.tt(T["klb"][:, 0:NW], T["klb"][:, 0:NW], T["klb"][:, 4:4 + NW],
                  ALU.subtract, w=[t3])
        t5 = k.tt(T["klb"][:, 0:NW], T["klb"][:, 0:NW], T["masks_s"][0:32, 4:8], ALU.mult,
                  w=[k.tok_consts, t4])
        k.dbg("dl", dl[:], tev)
        k.dbg("dlT", dlT[:], tevT)
        k.dbg("xnn", xnn[:], tx)
        k.dbg("klb", T["klb"][:, 0:NW], t5)
        k.tok_klb = t5
    k.barrier()


def ph3(k, psb):
    D, T = k.D, k.T
    with ExitStack() as st:
        g1 = k.sb("g1", (128, 4 * 36), F32, st)
        asm = k.sb("asm1", (128, 36), F32, st)
        t0 = k.mset(asm[:, 32:36], 0.0)
        t1 = k.cp(asm[:, 0:32], T["csum"][:])
        t2 = k.cp(asm[:, 32:33], T["maxp"][:, 0:1], w=[t0])
        td = k.dma("gpsimd", D["ag1_in"][:], asm[:, 0:36], w=[t1, t2])
        tc1 = k.coll("AllGather", ALU.bypass, RG_SEQ, D["ag1_in"][:],
                     D["ag1_out"][:], w=[td])
        tg = k.dma("gpsimd", g1[:].rearrange("p (j c) -> p j c", j=4),
                   D["ag1_out"].rearrange("(j p) c -> p j c", p=128), w=[tc1])
        mx = k.sb("mx1", (128, 4), F32, st)
        toks = [k.cp(mx[:, j:j + 1], g1[:, j * 36 + 32: j * 36 + 33], w=[tg])
                for j in range(4)]
        tmx = k.red(mx[:, 0:1], mx[:, 0:4], ALU.max, w=toks)
        ttr = k.tr(psb[7][0:1, 0:128], mx[:, 0:1], w=[tmx])
        tz = k.mset(T["scr8"][0:1, 1:9], 0.0)
        tm2 = k.red(T["scr8"][0:1, 1:2], psb[7][0:1, 0:128], ALU.max, w=[ttr, tz])
        tfull = k.dma("gpsimd", D["armax_in"][:], T["scr8"][0:1, 1:9], w=[tm2])
        tc2 = k.coll("AllReduce", ALU.max, RG_ALL, D["armax_in"][:],
                     D["armax_out"][:], w=[tfull])
        ts1 = k.dma("gpsimd", T["scr8"][0:1, 1:9], D["armax_out"][:], w=[tc2])
        tbc = k.mm(psb[7][:, 0:1], T["ones_row"][:], T["scr8"][0:1, 1:2], w=[ts1])
        tsc = k.cp(T["scal"][:, 1:2], psb[7][:, 0:1], w=[tbc])
        tq = k.ts(T["scal"][:, 3:4], T["scal"][:, 1:2], 1.0 / 32767.0, 1e-12,
                  ALU.mult, ALU.max, w=[tsc])
        tqi = k.ex("vector", lambda e: e.reciprocal(T["scal"][:, 2:3], T["scal"][:, 3:4]),
                   waits=[tq])
        k.tok_s1 = tqi
        k.dbg("s1", T["scal"][0:1, 1:4], tqi)
        cr = k.sb("carry_raw", (128, 32), F32, st)
        R1 = k.sb("R1", (128, 8), F32, st)
        tR = k.mset(R1[:], 0.0)
        tcr = None
        for w_ in range(NW):
            sl = cr[:, w_ * 8:(w_ + 1) * 8]
            tA = k.cp(sl, R1[:], w=[tR])
            for j in range(4):
                gj = g1[:, j * 36 + w_ * 8: j * 36 + (w_ + 1) * 8]
                tA = k.stt(sl, gj, T["masks_s"][:, j:j + 1], sl,
                           ALU.mult, ALU.add, w=[tg, tA, k.tok_consts])
                tR = k.tt(R1[:], R1[:], gj, ALU.add, w=[tg, tR])
            tcr = tA
        wth = k.sb("wth", (128, 2 * 1024), F32, st)
        fw = [None, None]
        mmt = None
        THH = 1024
        for kc in range(8):
            b = kc % 2
            tl = k.dma("sync", wth[:, b * THH:(b + 1) * THH],
                       D["WpT"][kc * 128:(kc + 1) * 128, OFF_THETA:OFF_THETA + THH],
                       w=[fw[b]], stream="wth%d" % b)
            lT = cr.rearrange("p (a b) -> p a b", b=8)[:, :, kc]
            m1 = k.mm(psb[6][0:NW, 0:512], lT, wth[:, b * THH: b * THH + 512],
                      start=(kc == 0), stop=(kc == 7), w=[tl, tcr])
            m2 = k.mm(psb[7][0:NW, 0:512], lT, wth[:, b * THH + 512: b * THH + 1024],
                      start=(kc == 0), stop=(kc == 7))
            fw[b] = m2
            mmt = m2
        k.dbg("carryraw", cr[:], tcr)
        cth = k.sb("cth", (NW, 1024), F32, st)
        te1 = k.cp(cth[:, 0:512], psb[6][0:NW, 0:512], w=[mmt])
        te2 = k.cp(cth[:, 512:1024], psb[7][0:NW, 0:512], w=[mmt])
        k.dbg("cth", cth[:], te2)
        prev_cp = {0: None, 1: None}
        tlast = None
        for fc in range(8):
            bk = 4 + (fc % 2)
            tt = k.tr(psb[bk][0:128, 0:NW], cth[:, fc * 128:(fc + 1) * 128],
                      w=[te1, te2, prev_cp[fc % 2]])
            for w_ in range(NW):
                tlast = k.cp(T["carryT"][:, w_ * 8 + fc: w_ * 8 + fc + 1],
                             psb[bk][0:128, w_:w_ + 1], w=[tt])
            prev_cp[fc % 2] = tlast
        k.dbg("carryT", T["carryT"][:], tlast)
    k.barrier()





def tc_slice(thw, fc, c0=None, n=None):
    base = 4096 if fc == 0 else (fc - 1) * 512
    if c0 is None:
        return thw[:, base:base + 512]
    return thw[:, base + c0: base + c0 + n]


def ph4(k, psb):
    D, T = k.D, k.T
    k.dint("ygp_sp", (TPC, DI))
    k.barrier()

    st = ExitStack()
    Rw = k.sb("Rw", (128, 1024), F32, st)
    vown = k.sb("vown", (128, 1024), F32, st)
    vown_bf = k.sb("vown_bf", (128, 1024), BF16, st)
    VLq = [k.sb("VL%d" % i, (128, 1024), F32, st) for i in range(2)]
    VLp_bf = k.sb("VLp_bf", (128, 1024), BF16, st)
    Sgt = k.sb("Sgt", (128, 1040), F32, st)
    cum = k.sb("cum", (128, 16), F32, st)
    apair = k.sb("apair", (128, 16), F32, st)
    xnw_bf = k.sb("xnw_bf", (128, 8 * SEG), BF16, st)
    AR1 = k.sb("AR1", (128, 4096), F32, st)
    wgs = k.sb("wgs", (128, 2 * 512), BF16, st)
    wfs = k.sb("wfs", (128, 2 * 128), BF16, st)
    thw = k.sb("thw", (128, 4608), F32, st)
    VEC1 = k.sb("VEC1", (128, 512), F32, st)
    VEC2 = k.sb("VEC2", (128, 512), F32, st)
    kltbuf = k.sb("klt", (96, SEG + 16 + 512), F32, st)
    klt = kltbuf[64:96, 0:SEG + 16]
    sten = k.sb("sten", (32, 16), F32, st)
    arow = k.sb("arow", (1, 32), F32, st)
    lbig = k.sb("lbig", (128, 1024), F32, st)
    sz = k.sb("szw", (128, 4 * DI), BF16, st)
    xh_bf = k.sb("xhbf", (128, 4 * DI), BF16, st)
    bc4 = k.sb("bc4", (128, 4 * 128), F32, st)
    bcn = k.sb("bcn", (128, 128), F32, st)
    CmeD = k.sb("CmeDw", (128, 4 * DI), BF16, st)
    tctm = k.sb("tctm", (128, 1024), F32, st)
    sinb = k.sb("sinb", (128, 1024), BF16, st)
    cosb = k.sb("cosb", (128, 1024), BF16, st)
    B2b = k.sb("B2b", (128, 4096), BF16, st)
    Bmt = k.sb("Bmt", (128, 2048), BF16, st)
    Cmt = k.sb("Cmt", (128, 2048), BF16, st)
    fmB = k.sb("fmB", (128, 2048), BF16, st)
    fmC = k.sb("fmC", (128, 2048), BF16, st)
    fmE = k.sb("fmE", (128, 2048), BF16, st)
    Ft = k.sb("Ftl", (128, 2048), BF16, st)
    Gb = k.sb("Gbt", (128, 2048), BF16, st)
    d0t = k.sb("d0t", (128, 32), F32, st)
    ktm = k.sb("ktm", (128, 32), F32, st)
    b1tm = k.sb("b1tm", (128, 32), F32, st)
    eLt = k.sb("eLt", (128, 32), F32, st)
    ctm = k.sb("ctm", (128, 32), F32, st)
    wal = k.sb("wal", (128, 32), F32, st)
    s1a = k.sb("s1a", (128, 1024), F32, st)
    s1b = k.sb("s1b", (128, 1024), F32, st)
    ysp_stage = k.sb("ysp_stage", (128, 256), BF16, st)
    rmss = k.sb("rmss", (128, 8), F32, st)

    dtr = VEC1[0:32, :]
    bias1 = VEC1[32:64, :]
    mdt = VEC1[64:96, :]
    lamt = kltbuf[64:96, SEG + 16:SEG + 16 + 512]
    delt = VEC2[0:32, :]
    ellt = VEC2[32:64, :]
    gtil = VEC2[64:96, :]
    lamr = VEC2[96:128, :]
    Bmc = B2b[:, 0:2048]
    Cme = B2b[:, 2048:4096]
    ygp = AR1[:, 0:2048]
    ywb = s1a[:, 0:1024].bitcast(BF16)

    k.mset(Rw[:], 0.0)
    k.mset(lbig[:], 0.0)
    pv = {}
    tok_ysp_all = []

    for w in range(NW):
        t_ld = k.dma("sync", AR1[:], D["xn_spill"].rearrange(
            "(a p) t -> p a t", p=128)[:, :, w * SEG:(w + 1) * SEG],
            w=[pv.get("AR1")], stream="xnw")
        t_cast = k.cp(xnw_bf[:], AR1[:], w=[t_ld, pv.get("xnwbf")])
        fbs = [("dt", OFF_DT, 32), ("lam", OFF_LAM, 32)] + \
            [("th%d" % j, OFF_THETA + 128 * j, 128) for j in range(8)]
        fw = [None, None]
        last_bk = {}
        ev_fm = []
        tscn = [None] * 8
        for fi, (nmf, col0, wid) in enumerate(fbs):
            bk = 5 + (fi % 3)
            bb = fi % 2
            mmt = None
            for kc in range(8):
                tl = k.dma("sync", wfs[:, bb * 128: bb * 128 + wid],
                           D["Wpbf"][kc * 128:(kc + 1) * 128, col0:col0 + wid],
                           w=[fw[bb], k.tok_wp, k.tok_wp0], stream="wf%d" % bb)
                mmt = k.mm(psb[bk][0:wid, 0:SEG],
                           wfs[:, bb * 128: bb * 128 + wid],
                           xnw_bf[:, kc * SEG:(kc + 1) * SEG],
                           start=(kc == 0), stop=(kc == 7),
                           w=[tl, t_cast, last_bk.get(bk), pv.get("pb%d" % bk)])
                fw[bb] = mmt
            if nmf == "dt":
                ev = k.cp(dtr, psb[bk][0:32, 0:SEG], w=[mmt, pv.get("vecf")],
                          eng="scalar")
            elif nmf == "lam":
                ev = k.cp(lamr, psb[bk][0:32, 0:SEG], w=[mmt, pv.get("vecf")],
                          eng="scalar")
            else:
                j = int(nmf[2:])
                ev = k.cp(thw[:, j * 512:(j + 1) * 512], psb[bk][0:128, 0:SEG],
                          w=[mmt, pv.get("thwf")], eng="scalar")
                tscn[j] = k.scan(tc_slice(thw, j),
                                 thw[:, j * 512:(j + 1) * 512],
                                 thw[:, j * 512:(j + 1) * 512],
                                 T["carryT"][:, w * 8 + j: w * 8 + j + 1],
                                 ALU.add, ALU.bypass,
                                 w=[ev, (tscn[j - 1] if j >= 1 else None),
                                    pv.get("thwf")])
            last_bk[bk] = ev
            ev_fm.append(ev)
        t1 = k.act(mdt, dtr, AF.Sigmoid, bias=T["small2"][:, 0:1],
                   scale=-1.0, w=ev_fm + [k.tok_small2])
        t1 = k.act(mdt, mdt, AF.Ln)
        t2 = k.act(lamt, lamr, AF.Sigmoid, w=ev_fm)
        t3 = k.ts(delt, mdt, T["small2"][:, 1:2], None, ALU.mult, w=[t1])
        t4 = k.scan(ellt, delt, delt, 0.0, ALU.add, ALU.bypass, w=[t3])
        t5 = k.tt(klt[:, 0:SEG], lamt, mdt, ALU.mult, w=[t1, t2])
        t6 = k.tt(klt[:, 0:SEG], klt[:, 0:SEG], mdt, ALU.subtract, w=[t5])
        t6b = k.cp(klt[:, SEG:SEG + 1], T["klb"][:, w:w + 1], w=[k.tok_klb])
        tg1 = k.tt(gtil, lamt, mdt, ALU.mult, w=[t1, t2])
        tg2 = k.tt(gtil, klt[:, 1:SEG + 1], gtil, ALU.subtract, w=[t6, t6b, tg1])
        tg3 = k.act(bias1, gtil, AF.Ln, w=[tg2])
        tg4 = k.tt(bias1, bias1, ellt, ALU.subtract, w=[t4, tg3])
        fw2 = [None, None]
        bank_free = {}
        ev_tm = []
        for og in range(9):
            if og < 4:
                col0, wid = og * 512, 512
            elif og < 8:
                col0, wid = 2048 + (og - 4) * 512, 512
            else:
                col0, wid = 4096, 128
            banks = [(og % 2) * 4 + i for i in range(4)]
            mmt = {}
            for kc in range(8):
                bb = og % 2
                tl = k.dma("sync", wgs[:, bb * 512: bb * 512 + wid],
                           D["Wpbf"][kc * 128:(kc + 1) * 128, col0:col0 + wid],
                           w=[fw2[bb]], stream="wg%d" % bb)
                for tb in range(4):
                    wts = [tl, t_cast]
                    if kc == 0:
                        wts += [bank_free.get(banks[tb]), pv.get("pb%d" % banks[tb]),
                                last_bk.get(banks[tb])]
                    mmt[tb] = k.mm(
                        psb[banks[tb]][0:128, 0:wid],
                        xnw_bf[:, kc * SEG + tb * 128: kc * SEG + (tb + 1) * 128],
                        wgs[:, bb * 512: bb * 512 + wid],
                        start=(kc == 0), stop=(kc == 7), w=wts)
                fw2[bb] = mmt[3]
            for tb in range(4):
                bk = banks[tb]
                if og < 4:
                    ev = k.act(sz[:, tb * DI + og * 512: tb * DI + og * 512 + 512],
                               psb[bk][0:128, 0:512], AF.Silu,
                               w=[mmt[tb], pv.get("szf")])
                elif og < 8:
                    ev = k.cp(xh_bf[:, tb * DI + (og - 4) * 512:
                                    tb * DI + (og - 4) * 512 + 512],
                              psb[bk][0:128, 0:512], w=[mmt[tb], pv.get("xhf")])
                else:
                    ev = k.cp(bc4[:, tb * 128:(tb + 1) * 128],
                              psb[bk][0:128, 0:128], w=[mmt[tb], pv.get("bcf")])
                bank_free[bk] = ev
                ev_tm.append(ev)
        t_cum = k.mset(cum[:], 1.0, w=[pv.get("cumf")])
        t_va = k.mset(vown[:], 0.0, w=[pv.get("vownf")])
        pb_free = dict(bank_free)
        ygp_w = []
        for c4 in range(4):
            ch0 = c4 * 128
            VLold = VLq[c4 % 2]
            VLnew = VLq[1 - c4 % 2]
            bcs = bc4[:, c4 * 128:(c4 + 1) * 128]
            tq = k.tt(bcn[:], bcs, bcs, ALU.mult, w=ev_tm)
            tq = k.red(rmss[:, 0:2], bcn[:].rearrange("p (a c) -> p a c", a=2),
                       ALU.add, w=[tq])
            tq = k.act(rmss[:, 2:4], rmss[:, 0:2], AF.Ln, bias=T["cvec"][:, 0:1],
                       scale=1.0 / 64.0, w=[tq])
            tq = k.act(rmss[:, 4:6], rmss[:, 2:4], AF.Exp, scale=-0.5, w=[tq])
            tb1 = k.ts(bcn[:, 0:64], bcs[:, 0:64], rmss[:, 4:5], None, ALU.mult,
                       w=[tq])
            tb2 = k.ts(bcn[:, 64:128], bcs[:, 64:128], rmss[:, 5:6], None,
                       ALU.mult, w=[tq])
            tb3 = k.tt(bcn[:], bcn[:], T["wbcbc"][:, 0:128], ALU.mult,
                       w=[tb1, tb2, k.tok_consts])
            tB2 = k.tt(AR1[:, 0:2048].rearrange("p (h n) -> p h n", h=32),
                       bcn[:, 0:64].unsqueeze(1).broadcast_to((128, 32, 64)),
                       T["biasbc"][:, 0:2048].rearrange("p (h n) -> p h n", h=32),
                       ALU.add, w=[tb3, t_cast, pv.get("ygpf")])
            tC2 = k.tt(AR1[:, 2048:4096].rearrange("p (h n) -> p h n", h=32),
                       bcn[:, 64:128].unsqueeze(1).broadcast_to((128, 32, 64)),
                       T["biasbc"][:, 2048:4096].rearrange("p (h n) -> p h n", h=32),
                       ALU.add, w=[tb3])
            tP1 = k.tt(s1a[:], AR1[:, 0:1024], AR1[:, 2048:3072], ALU.mult,
                       w=[tB2, tC2, pv.get("s1af")])
            tP2 = k.tt(s1b[:], AR1[:, 1024:2048], AR1[:, 3072:4096], ALU.mult,
                       w=[tB2, tC2, pv.get("s1bf")])
            td0a = k.red(d0t[:, 0:16], s1a[:].rearrange("p (h n) -> p h n", h=16),
                         ALU.add, w=[tP1])
            td0 = k.red(d0t[:, 16:32], s1b[:].rearrange("p (h n) -> p h n", h=16),
                        ALU.add, w=[tP2])
            tBb = k.cp(B2b[:, 0:2048], AR1[:, 0:2048],
                       w=[tB2, pv.get("B2bf"), pv.get("B2bfE")])
            tCb = k.cp(B2b[:, 2048:4096], AR1[:, 2048:4096],
                       w=[tC2, pv.get("B2bf"), pv.get("B2bfE")])
            ttc = []
            for fc in range(8):
                bkp = fc % 2
                tt0 = k.tr(psb[bkp][0:128, 0:128], tc_slice(thw, fc, ch0, 128),
                           w=[tscn[fc], (tscn[fc + 1] if fc + 1 < 8 else None),
                              pb_free.get(bkp),
                              ttc[fc - 2] if fc >= 2 else None])
                ttc.append(k.cp(tctm[:, fc * 128:(fc + 1) * 128],
                                psb[bkp][0:128, 0:128], w=[tt0], eng="scalar"))
            pb_free[0] = ttc[-2]
            pb_free[1] = ttc[-1]
            PIF = float(np.pi)
            ta1 = k.ts(s1a[:].bitcast(I32), tctm[:], INV2PI, 128.5, ALU.mult,
                       ALU.add, w=ttc + [td0a, td0])
            ta2 = k.ts(s1b[:], s1a[:].bitcast(I32), 1.0, -128.0, ALU.mult,
                       ALU.add, w=[ta1, tP2])
            ta2b = k.ts(s1b[:], s1b[:], -TWOPI, None, ALU.mult)
            ta3 = k.tt(s1b[:], s1b[:], tctm[:], ALU.add, eng="gpsimd", w=[ta2b])
            ta3m = k.ts(s1a[:], s1b[:], -PIF, None, ALU.is_lt, w=[ta3])
            ta3n = k.stt(s1b[:], s1a[:], TWOPI, s1b[:], ALU.mult, ALU.add,
                         w=[ta3m])
            ta3b = k.ts(s1b[:], s1b[:], -PIF, PIF, ALU.max, ALU.min, w=[ta3n])
            ts1 = k.act(sinb[:], s1b[:], AF.Sin, w=[ta3b])
            ta4 = k.ts(s1a[:].bitcast(I32), tctm[:], INV2PI, 128.75, ALU.mult,
                       ALU.add, w=[ta2])
            ta5 = k.ts(s1b[:], s1a[:].bitcast(I32), 1.0, -128.0, ALU.mult,
                       ALU.add, w=[ta4, ts1, ta3b])
            ta5b = k.ts(s1b[:], s1b[:], -TWOPI, HALFPI, ALU.mult, ALU.add)
            ta6 = k.tt(s1b[:], s1b[:], tctm[:], ALU.add, eng="gpsimd", w=[ta5b])
            ta6m = k.ts(s1a[:], s1b[:], -PIF, None, ALU.is_lt, w=[ta6])
            ta6n = k.stt(s1b[:], s1a[:], TWOPI, s1b[:], ALU.mult, ALU.add,
                         w=[ta6m])
            ta6b = k.ts(s1b[:], s1b[:], -PIF, PIF, ALU.max, ALU.min, w=[ta6n])
            ts2 = k.act(cosb[:], s1b[:], AF.Sin, w=[ta6b])
            ta5 = ta5b
            ta6 = ta6b
            def rope(dst, off, wt):
                v1 = B2b[:, off:off + 2048].rearrange("p (h n) -> p h n",
                                                      h=32)[:, :, 0:32]
                v2 = B2b[:, off:off + 2048].rearrange("p (h n) -> p h n",
                                                      h=32)[:, :, 32:64]
                cs = cosb[:].rearrange("p (h f) -> p h f", h=32)
                sn = sinb[:].rearrange("p (h f) -> p h f", h=32)
                d3 = dst.rearrange("p (h n) -> p h n", h=32)
                sA = s1a[:].rearrange("p (h f) -> p h f", h=32)
                sB = s1b[:].rearrange("p (h f) -> p h f", h=32)
                r1 = k.tt(d3[:, :, 0:32], v1, cs, ALU.mult, w=[ts1, ts2, wt])
                r2 = k.tt(sA, v2, sn, ALU.mult, eng="gpsimd",
                          w=[ts1, ts2, wt, ta5])
                r3 = k.tt(d3[:, :, 0:32], d3[:, :, 0:32], sA, ALU.subtract,
                          w=[r1, r2])
                r4 = k.tt(sB, v1, sn, ALU.mult, eng="gpsimd", w=[ts1, ts2, wt, ta6])
                r5 = k.tt(d3[:, :, 32:64], v2, cs, ALU.mult, w=[ts1, ts2, wt])
                r6 = k.tt(d3[:, :, 32:64], d3[:, :, 32:64], sB, ALU.add,
                          w=[r4, r5])
                return r6
            tBm = rope(Bmt[:], 0, tBb)
            tCm = rope(Cmt[:], 2048, tCb)
            if ch0 > 0:
                tprev = k.ts(sten[:, 0:1], ellt[:, ch0 - 1:ch0], -1.0, None,
                             ALU.mult, w=[t4])
            else:
                tprev = k.mset(sten[:, 0:1], 0.0, w=[t4])
            teL = k.act(s1a[0:32, 0:128], ellt[:, ch0:ch0 + 128], AF.Exp,
                        bias=sten[:, 0:1], w=[tprev, tBm, tCm])
            ttE = k.tr(psb[2][0:128, 0:32], s1a[0:32, 0:128],
                       w=[teL, pb_free.get(2), pv.get("pb2")])
            teL2 = k.cp(eLt[:], psb[2][0:128, 0:32], w=[ttE])
            ttA = k.tr(psb[3][0:1, 0:32], s1a[0:32, 127:128],
                       w=[teL, pb_free.get(3), pv.get("pb3")])
            tAr = k.cp(arow[:], psb[3][0:1, 0:32], w=[ttA])
            tcf = k.act(s1a[0:32, 128:256], bias1[:, ch0:ch0 + 128], AF.Exp,
                        bias=ellt[:, ch0 + 127:ch0 + 128], w=[tg4, teL])
            tct = k.tr(psb[2][0:128, 32:64], s1a[0:32, 128:256], w=[tcf, ttE])
            tct2 = k.cp(ctm[:], psb[2][0:128, 32:64], w=[tct])
            tks = k.tr(psb[3][0:128, 32:64], klt[:, ch0 + 1:ch0 + 129],
                       w=[t6, t6b, ttA])
            tkt = k.cp(ktm[:], psb[3][0:128, 32:64], w=[tks])
            tbt = k.tr(psb[2][0:128, 64:96], VEC1[32:64, ch0:ch0 + 128],
                       w=[tg4, tct])
            tbt2 = k.cp(b1tm[:], psb[2][0:128, 64:96], w=[tbt])
            pb_free[2] = tbt2
            tw1 = k.tt(wal[:], ktm[:], d0t[:], ALU.mult, w=[tkt, td0, td0a])
            tw2 = k.tt(wal[:], T["Drow"][:], wal[:], ALU.subtract, w=[tw1])
            tCme = k.tt(Cme.rearrange("p (h n) -> p h n", h=32),
                        Cmt[:].rearrange("p (h n) -> p h n", h=32),
                        eLt[:].unsqueeze(2).broadcast_to((128, 32, 64)),
                        ALU.mult, w=[tCm, teL2, tCb])
            tBmc = k.tt(Bmc.rearrange("p (h n) -> p h n", h=32),
                        Bmt[:].rearrange("p (h n) -> p h n", h=32),
                        ctm[:].unsqueeze(2).broadcast_to((128, 32, 64)),
                        ALU.mult, w=[tBm, tct2, tBb])
            prev_fm = {}
            tok_fm = {}
            for (nm, srct, dstt, wt) in [("B", Bmt[:], fmB, tBm),
                                         ("C", Cmt[:], fmC, tCm),
                                         ("E", Cme, fmE, tCme)]:
                evl = None
                for pr in range(16):
                    bkp = 6 + pr % 2
                    pbv = psb[bkp][:].bitcast(BF16)[0:128, 0:128]
                    tt0 = k.tr(pbv, srct[:, pr * 128:(pr + 1) * 128], T["idnb"][:],
                               w=[wt, prev_fm.get(bkp), pb_free.get(bkp)])
                    if pr < 2:
                        pb_free[bkp] = None
                    evl = k.cp(dstt[:, pr * 128:(pr + 1) * 128],
                               pbv, w=[tt0], eng="scalar")
                    prev_fm[bkp] = evl
                tok_fm[nm] = evl
            pb_free[6] = prev_fm[6]
            pb_free[7] = prev_fm[7]
            tcd = None
            for pr in range(16):
                tcd = k.ts(CmeD[:, c4 * DI + pr * 128: c4 * DI + (pr + 1) * 128],
                           fmE[:, pr * 128:(pr + 1) * 128],
                           cum[:, pr:pr + 1], None, ALU.mult,
                           w=[tok_fm["E"], t_cum, pv.get("cmedf")])
            arv = arow[:].rearrange("p (a b) -> p a b", b=2)
            tap1 = k.mm(psb[3][0:128, 64:80], T["rows3_s"][0:1, 128:256],
                        arv[:, :, 0], start=True, stop=False,
                        w=[tAr, tks, k.tok_consts])
            tap2 = k.mm(psb[3][0:128, 64:80], T["rows3_s"][0:1, 256:384],
                        arv[:, :, 1], start=False, stop=True)
            tap3 = k.cp(apair[:], psb[3][0:128, 64:80], w=[tap2])
            pb_free[3] = tap3
            pv["pb3"] = None
            smm = []
            for pr in range(16):
                bkS = 4 + pr // 8
                colS = (pr % 8) * 64
                wts = [tBmc]
                if pr % 8 == 0:
                    wts += [pb_free.get(bkS), pv.get("pb%d" % bkS)]
                    pb_free[bkS] = None
                    pv["pb%d" % bkS] = None
                m1 = k.mm(psb[bkS][0:64, colS:colS + 64],
                          Bmc[:, pr * 128: pr * 128 + 64],
                          xh_bf[:, c4 * DI + pr * 128: c4 * DI + pr * 128 + 64],
                          tp=(0, 0), w=wts)
                m2 = k.mm(psb[bkS][64:128, colS:colS + 64],
                          Bmc[:, pr * 128 + 64: pr * 128 + 128],
                          xh_bf[:, c4 * DI + pr * 128 + 64: c4 * DI + pr * 128 + 128],
                          tp=(0, 64))
                smm.append(m2)
            if c4 == 0:
                tVb = k.mset(VLp_bf[:], 0.0, w=[pv.get("VLbf")])
            else:
                tVb = k.cp(VLp_bf[:], VLold[:], w=[pv.get("VLbf")])
            tvl = []
            for pr in range(16):
                bkS = 4 + pr // 8
                colS = (pr % 8) * 64
                half = psb[bkS][0:128, colS:colS + 64]
                if c4 == 0:
                    tvl.append(k.cp(VLnew[:, pr * 64:(pr + 1) * 64], half,
                                    w=[smm[pr], pv.get("VLfree")]))
                else:
                    tvl.append(k.stt(VLnew[:, pr * 64:(pr + 1) * 64],
                                     VLold[:, pr * 64:(pr + 1) * 64],
                                     apair[:, pr:pr + 1], half,
                                     ALU.mult, ALU.add,
                                     w=[smm[pr], tap3, pv.get("VLfree")]))
            t_cum = k.tt(cum[:], cum[:], apair[:], ALU.mult, w=[tap3, tcd])
            evy_all = []
            tlb_prev = pv.get("lbrf")
            gmm = {}
            dmm = {}
            for h2 in range(2):
                hh0 = h2 * 16
                for g8 in range(2):
                    h0 = hh0 + g8 * 8
                    tlb = k.dma("gpsimd", lbig[0:1, 0:1024],
                                ellt[h0:h0 + 8, ch0:ch0 + 128],
                                w=[t4, tlb_prev], stream="lbr")
                    for qi in range(8):
                        q = g8 * 8 + qi
                        h = hh0 + q
                        pr, half = h // 2, h % 2
                        bkG = q // 4
                        colG = (q % 4) * 128
                        bkD = 4 + q // 4
                        prt = slice(half * 64, half * 64 + 64)
                        wtsG = [tok_fm["B"], tok_fm["C"]]
                        if q % 4 == 0:
                            wtsG += [pb_free.get(bkG), tvl[-1], ttc[2 * bkG]]
                            pb_free[bkG] = None
                        gmm[q] = k.mm(psb[bkG][0:128, colG:colG + 128],
                                      fmB[:, pr * 128:(pr + 1) * 128][prt, :],
                                      fmC[:, pr * 128:(pr + 1) * 128][prt, :],
                                      tp=(64 * half, 0), w=wtsG)
                        wtsD = [tlb]
                        if q % 4 == 0:
                            wtsD += [pb_free.get(bkD), tvl[-1]]
                            pb_free[bkD] = None
                        dmm[q] = k.mm(psb[bkD][0:128, colG:colG + 128],
                                      T["e0mat_s"][:],
                                      lbig[:, qi * 128:(qi + 1) * 128],
                                      start=(q % 4 == 0), stop=False, w=wtsD)
                    for q in range(g8 * 8, g8 * 8 + 8, 4):
                        bkD = 4 + q // 4
                        colG0 = (q % 4) * 128
                        dmm[q + 3] = k.mm(psb[bkD][0:128, 0:512],
                                          T["mska_s"][:], T["mskb4_s"][:],
                                          start=False, stop=True,
                                          w=[k.tok_consts])
                    tlb_prev = dmm[g8 * 8 + 7]
                evG = []
                evF = {}
                for bk in range(4):
                    evG.append(k.cp(Gb[:, bk * 512:(bk + 1) * 512],
                                    psb[bk][0:128, 0:512],
                                    w=[gmm[hh0 % 16 + bk * 4 + 3] if False else
                                       gmm[bk * 4 + 3 + 0]], eng="scalar"))
                for q in range(16):
                    h = hh0 + q
                    bkD = 4 + q // 4
                    colG = (q % 4) * 128
                    evF[q] = k.act(Ft[:, q * 128:(q + 1) * 128],
                                   psb[bkD][0:128, colG:colG + 128], AF.Exp,
                                   bias=b1tm[:, h:h + 1],
                                   w=[dmm[(q // 4) * 4 + 3], tbt2])
                tW1 = k.tt(Ft[:], Ft[:], Gb[:], ALU.mult,
                            w=list(evF.values()) + evG)
                ymm = {}
                cmm = {}
                for q in range(16):
                    h = hh0 + q
                    pr, half = h // 2, h % 2
                    bkY = q // 8
                    colY = (q % 8) * 64
                    prt = slice(half * 64, half * 64 + 64)
                    ymm[q] = k.mm(psb[bkY][0:128, colY:colY + 64],
                                  Ft[:, q * 128:(q + 1) * 128],
                                  xh_bf[:, c4 * DI + h * 64: c4 * DI + h * 64 + 64],
                                  w=[tW1] + (evG if q % 8 == 0 else []))
                    cmm[q] = k.mm(psb[4 + bkY][0:128, colY:colY + 64],
                                  fmE[:, pr * 128:(pr + 1) * 128][prt, :],
                                  VLp_bf[prt, pr * 64:(pr + 1) * 64],
                                  tp=(64 * half, 0),
                                  w=[tok_fm["E"], tVb] +
                                  (list(evF.values()) if q % 8 == 0 else []))
                for bq in range(2):
                    c0 = h2 * 1024 + bq * 512
                    e1 = k.cp(s1a[:, bq * 512:(bq + 1) * 512],
                              psb[4 + bq][0:128, 0:512],
                              w=[cmm[bq * 8 + 7]], eng="scalar")
                    e2 = k.stt(s1b[:, bq * 512:(bq + 1) * 512],
                               psb[bq][0:128, 0:512], 0.0,
                               s1a[:, bq * 512:(bq + 1) * 512],
                               ALU.add, ALU.add, w=[ymm[bq * 8 + 7], e1])
                    hw0 = hh0 + bq * 8
                    e3 = k.tt(s1a[:, bq * 512:(bq + 1) * 512].rearrange(
                        "p (a c) -> p a c", a=8),
                        xh_bf[:, c4 * DI + hw0 * 64: c4 * DI + (hw0 + 8) * 64]
                        .rearrange("p (a c) -> p a c", a=8),
                        wal[:, hw0:hw0 + 8].unsqueeze(2).broadcast_to((128, 8, 64)),
                        ALU.mult, eng="gpsimd", w=[tw2, e2])
                    e4 = k.tt(s1b[:, bq * 512:(bq + 1) * 512],
                              s1b[:, bq * 512:(bq + 1) * 512],
                              s1a[:, bq * 512:(bq + 1) * 512], ALU.add, w=[e3, e2])
                    e5 = k.tt(ygp[:, c0:c0 + 512], s1b[:, bq * 512:(bq + 1) * 512],
                              sz[:, c4 * DI + c0: c4 * DI + c0 + 512], ALU.mult,
                              w=[e4, pv.get("ygpf"), tBb, tCb, tP1, tP2])
                    evy_all.append(e5)
                pb_free[0] = evy_all[-1]
                pb_free[1] = evy_all[-1]
                pb_free[2] = evG[2]
                pb_free[3] = evG[3]
                pb_free[4] = evy_all[-1]
                pb_free[5] = evy_all[-1]
                pv["pb2"] = None
            tok_sp = k.dma("scalar",
                           D["ygp_sp"][w * SEG + ch0: w * SEG + ch0 + 128, :],
                           ygp[:], w=evy_all, stream="ygp")
            ygp_w.append(tok_sp)
            pv["ygpf"] = tok_sp
            pv["VLfree"] = evy_all[-1]
            pv["VLbf"] = cmm[15]
            pv["s1af"] = evy_all[-1]
            pv["s1bf"] = evy_all[-1]
            pv["B2bf"] = tvl[-1]
            pv["B2bfE"] = tok_fm["E"]
            pv["lbrf"] = tlb_prev
        tpk1 = k.cp(Sgt[:, 0:1024], VLq[0][:], w=tvl)
        tpk2 = k.cp(Sgt[:, 1024:1040], cum[:], w=[t_cum])
        tdi = k.dma("gpsimd", D["ag2_in%d" % w][:], Sgt[:, 0:1040],
                    w=[tpk1, tpk2], stream="ag2")
        tco = k.coll("AllGather", ALU.bypass, RG_SEQ, D["ag2_in%d" % w][:],
                     D["ag2_out%d" % w][:], w=[tdi])
        tfold = None
        for j in range(4):
            tgj = k.dma("gpsimd", Sgt[:],
                        D["ag2_out%d" % w][j * 128:(j + 1) * 128, :],
                        w=[tco, tfold], stream="sgl")
            tc1 = k.stt(vown[:], Rw[:], T["masks_s"][:, 8 + j: 9 + j], vown[:],
                        ALU.mult, ALU.add, w=[tgj, t_va, k.tok_consts])
            tc2 = k.tt(s1a[:].rearrange("p (a c) -> p a c", a=16),
                       Rw[:].rearrange("p (a c) -> p a c", a=16),
                       Sgt[:, 1024:1040].unsqueeze(2).broadcast_to((128, 16, 64)),
                       ALU.mult, w=[tgj, tc1, pv.get("s1af")])
            tc3 = k.tt(Rw[:], s1a[:], Sgt[:, 0:1024], ALU.add, w=[tc2])
            pv["s1af"] = tc3
            tfold = tc3
        tvbf = k.cp(vown_bf[:], vown[:], w=[tfold])
        pv["vownf"] = tvbf
        ysp_last = {}
        for c4 in range(4):
            ch0 = c4 * 128
            cg = {}
            for q in range(32):
                pr, half = q // 2, q % 2
                bkC = q // 8
                colC = (q % 8) * 64
                prt = slice(half * 64, half * 64 + 64)
                wts = [tvbf, tcd]
                if q % 8 == 0:
                    wts += [pb_free.get(bkC), pv.get("pbc%d" % bkC)]
                    pb_free[bkC] = None
                cg[q] = k.mm(psb[bkC][0:128, colC:colC + 64],
                             CmeD[:, c4 * DI + pr * 128: c4 * DI + (pr + 1) * 128]
                             [prt, :],
                             vown_bf[prt, pr * 64:(pr + 1) * 64],
                             tp=(64 * half, 0), w=wts)
            tld = k.dma("sync", ygp[:],
                        D["ygp_sp"][w * SEG + ch0: w * SEG + ch0 + 128, :],
                        w=[ygp_w[c4], pv.get("ygprd"), pv.get("AR1f")],
                        stream="ygprd")
            ev4 = []
            for bkC in range(4):
                eA = k.tt(s1b[:, (bkC % 2) * 512:(bkC % 2) * 512 + 512],
                          psb[bkC][0:128, 0:512],
                          sz[:, c4 * DI + bkC * 512: c4 * DI + (bkC + 1) * 512],
                          ALU.mult, w=[cg[bkC * 8 + 7], pv.get("s1bf")])
                eB = k.tt(ygp[:, bkC * 512:(bkC + 1) * 512],
                          ygp[:, bkC * 512:(bkC + 1) * 512],
                          s1b[:, (bkC % 2) * 512:(bkC % 2) * 512 + 512],
                          ALU.add, w=[eA, tld])
                ev4.append(eB)
                pv["s1bf"] = eB
                pv["pbc%d" % bkC] = eB
            tsq = k.tt(s1b[:], ygp[:, 0:1024], ygp[:, 0:1024], ALU.mult, w=ev4)
            tr1 = k.red(rmss[:, 0:1], s1b[:], ALU.add, w=[tsq])
            tsq2 = k.tt(s1b[:], ygp[:, 1024:2048], ygp[:, 1024:2048], ALU.mult,
                        w=[tr1])
            tr2 = k.red(rmss[:, 1:2], s1b[:], ALU.add, w=[tsq2])
            tr3 = k.tt(rmss[:, 2:3], rmss[:, 0:1], rmss[:, 1:2], ALU.add,
                       w=[tr1, tr2])
            tr4 = k.act(rmss[:, 3:4], rmss[:, 2:3], AF.Ln, bias=T["cvec"][:, 0:1],
                        scale=1.0 / DI, w=[tr3])
            tr5 = k.act(rmss[:, 4:5], rmss[:, 3:4], AF.Exp, scale=-0.5, w=[tr4])
            ty1 = k.ts(ygp[:], ygp[:], rmss[:, 4:5], None, ALU.mult,
                       w=[tr5, tsq2])
            ty2 = k.tt(ywb, ygp[:], T["wnbc"][:], ALU.mult,
                       w=[ty1, k.tok_consts, pv.get("s1af")])
            pv["s1af"] = ty2
            pv["s1bf"] = ty2
            evsp = []
            for fc in range(16):
                bkp = 4 + fc % 2
                pbv = psb[bkp][:].bitcast(BF16)[0:128, 0:128]
                tt0 = k.tr(pbv, ywb[:, fc * 128:(fc + 1) * 128], T["idnb"][:],
                           w=[ty2, ysp_last.get(fc % 2), pb_free.get(bkp)])
                if fc < 2:
                    pb_free[bkp] = None
                ev = k.cp(ysp_stage[:, (fc % 2) * 128:(fc % 2) * 128 + 128],
                          pbv, w=[tt0, ysp_last.get(fc % 2)], eng="scalar")
                tsp2 = k.dma("scalar",
                             D["y_sp"][fc * 128:(fc + 1) * 128,
                                       w * SEG + ch0: w * SEG + ch0 + 128],
                             ysp_stage[:, (fc % 2) * 128:(fc % 2) * 128 + 128],
                             w=[ev], stream="ysp%d" % (fc % 2))
                ysp_last[fc % 2] = tsp2
                evsp.append(tsp2)
            pb_free[4] = evsp[-2]
            pb_free[5] = evsp[-1]
            pv["ygprd"] = ty2
            tok_ysp_all.extend(evsp[-2:])
        pv["AR1"] = ty2
        pv["AR1f"] = ty2
        pv["ygpf"] = ty2
        pv["vecf"] = tg4
        pv["thwf"] = ttc[-1]
        pv["xnwbf"] = None
        pv["szf"] = ev4[-1]
        pv["xhf"] = evy_all[-1]
        pv["bcf"] = tb3
        pv["cumf"] = tpk2
        pv["cmedf"] = cg[31]
        for bq in range(4):
            pv["pb%d" % bq] = cg[bq * 8 + 7]
        pv["pb4"] = evsp[-2]
        pv["pb5"] = evsp[-1]
        pv["pb6"] = pb_free[6]
        pv["pb7"] = pb_free[7]
    k.tok_ysp = tok_ysp_all
    k.dbg("ellt_w3", ellt[:, 0:SEG], t4)
    k.dbg("d0_w3c3", d0t[:], td0)
    k.dbg("tctm_w3c3", tctm[:], ttc[-1])
    k.dbg("Bm_w3c3", Bmt[:], tBm)
    k.dbg("sin_w3c3", sinb[:], ts1)
    k.dbg("cos_w3c3", cosb[:], ts2)
    k.dbg("B2_w3c3", AR1[:, 0:2048], tB2)
    k.dbg("Cm_w3c3", Cmt[:], tCm)
    k.dbg("xh_w3", xh_bf[:], evy_all[-1])
    k.dbg("yw_w3c3", ygp[:], ty1)
    k.dbg("Rw_fin", Rw[:], tfold)
    k.dbg("VLfin_w3", VLq[0][:], tpk1)
    k.dbg("vown_w3", vown[:], tvbf)
    k.dbg("cum_w3", cum[:], t_cum)
    k.dbg("apair_w3c3", apair[:], tap3)
    k.barrier()
    st.close()


QMAX = 32767.0

def _dequant_weight(k, psb, st, src_d, dst_bf, nrow, ncol, tag, dst_dram=None):
    T = k.T
    wld = k.sb("wld" + tag, (128, 2 * ncol), F32, st)
    bmx = k.sb("bmx" + tag, (128, ncol // 32), F32, st)
    wq = k.sb("wq" + tag, (128, ncol), I32, st)
    wdq = k.sb("wdq" + tag, (128, ncol), BF16, st)
    if dst_dram is not None:
        wstg = k.sb("wstg" + tag, (128, 256), BF16, st)
    nblk = ncol // 32
    for it in range(nrow // 128):
        b = it % 2
        wsl = wld[:, b * ncol:(b + 1) * ncol]
        k.dma("sync", wsl, src_d[it * 128:(it + 1) * 128, :], stream="wdq%d" % b)
        k.red(bmx[:], wsl.rearrange("p (a c) -> p a c", c=32), ALU.max, absv=True)
        k.ts(bmx[:], bmx[:], 1.0 / 7.0, 1e-12, ALU.mult, ALU.max)
        k.ex("vector", lambda e, s=bmx: e.reciprocal(s[:], s[:]),
             rd=[bmx[:]], wr=[bmx[:]])
        k.tt(wq[:].rearrange("p (a c) -> p a c", c=32),
             wsl.rearrange("p (a c) -> p a c", c=32),
             bmx[:].unsqueeze(2).broadcast_to((128, nblk, 32)), ALU.mult)
        k.ex("vector", lambda e, s=bmx: e.reciprocal(s[:], s[:]),
             rd=[bmx[:]], wr=[bmx[:]])
        k.tt(wdq[:].rearrange("p (a c) -> p a c", c=32),
             wq[:].rearrange("p (a c) -> p a c", c=32),
             bmx[:].unsqueeze(2).broadcast_to((128, nblk, 32)), ALU.mult)
        for fc in range(ncol // 128):
            bkp = fc % 2
            pbv = psb[bkp][:].bitcast(BF16)[0:128, 0:128]
            k.tr(pbv, wdq[:, fc * 128:(fc + 1) * 128], T["idnb"][:])
            if dst_dram is None:
                k.cp(dst_bf[:, fc * nrow + it * 128: fc * nrow + (it + 1) * 128],
                     pbv, eng="scalar")
            else:
                k.cp(wstg[:, (fc % 2) * 128:(fc % 2) * 128 + 128], pbv,
                     eng="scalar")
                k.dma("scalar",
                      dst_dram[fc * 128:(fc + 1) * 128,
                               it * 128:(it + 1) * 128],
                      wstg[:, (fc % 2) * 128:(fc % 2) * 128 + 128],
                      stream="wdqs%d" % (fc % 2))


def _finalize_max(k, psb, part_ap, ar_in, ar_out, scal_cols):
    T = k.T
    nvals = part_ap.shape[1]
    k.tr(psb[7][0:nvals, 0:128], part_ap)
    k.red(T["scr8"][0:nvals, 9:10], psb[7][0:nvals, 0:128], ALU.max)
    k.tr(psb[7][0:1, 128:128 + nvals], T["scr8"][0:nvals, 9:10])
    k.mset(T["scr8"][0:1, 0:8], 0.0)
    k.cp(T["scr8"][0:1, 0:nvals], psb[7][0:1, 128:128 + nvals])
    k.dma("gpsimd", ar_in[:], T["scr8"][0:1, 0:8], stream="armx")
    k.coll("AllReduce", ALU.max, RG_ALL, ar_in[:], ar_out[:])
    k.dma("gpsimd", T["scr8"][0:1, 0:8], ar_out[:], stream="armx")
    for j, (c_inv, c_scale) in enumerate(scal_cols):
        k.mm(psb[7][:, 128:129], T["ones_row"][:], T["scr8"][0:1, j:j + 1])
        k.ts(T["scal"][:, c_scale:c_scale + 1], psb[7][:, 128:129],
             1.0 / QMAX, 1e-12, ALU.mult, ALU.max)
        k.ex("vector", lambda e, ci=c_inv, cs=c_scale: e.reciprocal(
            T["scal"][:, ci:ci + 1], T["scal"][:, cs:cs + 1]),
            rd=[T["scal"][:, c_scale:c_scale + 1]],
            wr=[T["scal"][:, c_inv:c_inv + 1]])


def ph5(k, psb):
    D, T = k.D, k.T
    with ExitStack() as st:
        WgT = k.sb("WgT", (128, 8 * IFF), BF16, st)
        WuT = k.sb("WuT", (128, 8 * IFF), BF16, st)
        _dequant_weight(k, psb, st, D["Wg"], WgT, IFF, DM, "g")
        _dequant_weight(k, psb, st, D["Wu"], WuT, IFF, DM, "u")
        xnt = k.sb("xnt5", (128, 4096), F32, st)
        xq = k.sb("xq5", (128, 4096), I32, st)
        xqb = k.sb("xqb5", (128, 4096), BF16, st)
        stg = k.sb("stg5", (128, 4 * 512), F32, st)
        mxp = k.sb("mxp5", (128, 3), F32, st)
        k.mset(mxp[:], 0.0)
        for tt in range(4):
            k.dma("sync", xnt[:], D["xn_spill"].rearrange(
                "(a p) t -> p a t", p=128)[:, :, tt * SEG:(tt + 1) * SEG],
                stream="xnl")
            k.ts(xq[:], xnt[:], T["scal"][:, 2:3], None, ALU.mult, w=[k.tok_s1])
            k.ts(xqb[:], xq[:], T["scal"][:, 3:4], None, ALU.mult)
            for gu in range(2):
                WT = WgT if gu == 0 else WuT
                spd = D["gate_sp"] if gu == 0 else D["up_sp"]
                for oc in range(24):
                    bk = oc % 4
                    sb_ = (oc % 4)
                    for kc in range(8):
                        k.mm(psb[bk][0:128, 0:512],
                             WT[:, kc * IFF + oc * 128: kc * IFF + (oc + 1) * 128],
                             xqb[:, kc * SEG:(kc + 1) * SEG],
                             start=(kc == 0), stop=(kc == 7))
                    k.cp(stg[:, sb_ * 512:(sb_ + 1) * 512], psb[bk][0:128, 0:512],
                         eng="scalar")
                    k.dma("scalar", spd[oc * 128:(oc + 1) * 128,
                                        tt * SEG:(tt + 1) * SEG].rearrange(
                                            "p t -> p t"),
                          stg[:, sb_ * 512:(sb_ + 1) * 512],
                          stream="sp%d" % sb_)
                    k.red(mxp[:, 2:3], stg[:, sb_ * 512:(sb_ + 1) * 512],
                          ALU.max, absv=True)
                    k.tt(mxp[:, gu:gu + 1], mxp[:, gu:gu + 1], mxp[:, 2:3],
                         ALU.max)
        _finalize_max(k, psb, mxp[:, 0:2], D["ar23_in"], D["ar23_out"],
                      [(6, 7), (4, 5)])
    k.barrier()


def ph6(k, psb):
    D, T = k.D, k.T
    k.dbg("ysp_all", D["y_sp"][:], None, BF16)
    k.dbg("gate_sp0", D["gate_sp"][0:128, :], None)
    k.dbg("up_sp0", D["up_sp"][0:128, :], None)
    with ExitStack() as st:
        WoT = k.sb("WoT_bf", (128, 16 * DM), BF16, st)
        wst = k.sb("wst6", (128, 2 * DM), F32, st)
        ybf = k.sb("ybf6", (128, 16 * 512), BF16, st)
        sst = k.sb("sst6", (128, 512), F32, st)
        for kc in range(16):
            b = kc % 2
            k.dma("sync", wst[:, b * DM:(b + 1) * DM],
                  D["WoT"][kc * 128:(kc + 1) * 128, :], stream="wo%d" % b)
            k.cp(WoT[:, kc * DM:(kc + 1) * DM], wst[:, b * DM:(b + 1) * DM])
        for wv in range(4):
            k.dma("sync", ybf[:], D["y_sp"].rearrange(
                "(a p) t -> p a t", p=128)[:, :, wv * SEG:(wv + 1) * SEG],
                stream="yld")
            for oc in range(8):
                bk = oc % 4
                for kc in range(16):
                    k.mm(psb[bk][0:128, 0:512],
                         WoT[:, kc * DM + oc * 128: kc * DM + (oc + 1) * 128],
                         ybf[:, kc * 512:(kc + 1) * 512],
                         start=(kc == 0), stop=(kc == 15))
                k.cp(sst[:], psb[bk][0:128, 0:512], eng="scalar")
                k.dma("scalar", D["s_sp"][oc * 128:(oc + 1) * 128,
                                          wv * SEG:(wv + 1) * SEG], sst[:],
                      stream="ssp%d" % (oc % 2))
    k.barrier()


def _qdq_chain_tile(k, src_sb, inv_col, scale_col, out_sb):
    T = k.T
    k.ts(out_sb.bitcast(I32), src_sb, T["scal"][:, inv_col:inv_col + 1], None,
         ALU.mult)
    k.ts(out_sb, out_sb.bitcast(I32), T["scal"][:, scale_col:scale_col + 1],
         None, ALU.mult)


def ph7(k, psb):
    D, T = k.D, k.T
    with ExitStack() as st:
        g1 = k.sb("g7", (128, 2 * 2048), F32, st)
        gq = k.sb("gq7", (128, 2048), F32, st)
        sg7 = k.sb("sg7", (128, 2048), F32, st)
        tv = k.sb("tv7", (128, 2048), F32, st)
        mx = k.sb("mx7", (128, 2), F32, st)
        k.mset(mx[:], 0.0)
        for it in range(24):
            b = it % 2
            gsl = g1[:, b * 2048:(b + 1) * 2048]
            k.dma("sync", gsl, D["gate_sp"][it * 128:(it + 1) * 128, :],
                  stream="g7%d" % b)
            _qdq_chain_tile(k, gsl, 6, 7, gq[:])
            k.act(sg7[:], gq[:], AF.Sigmoid)
            k.ts(sg7[:].bitcast(I32), sg7[:], 65536.0, None, ALU.mult)
            k.ts(sg7[:], sg7[:].bitcast(I32), 65535.0, 1.0 / 65536.0,
                 ALU.min, ALU.mult)
            k.tt(tv[:], gq[:], sg7[:], ALU.mult)
            k.dma("scalar", D["t_sp"][it * 128:(it + 1) * 128, :], tv[:],
                  stream="t7")
            k.red(mx[:, 1:2], tv[:], ALU.max, absv=True)
            k.tt(mx[:, 0:1], mx[:, 0:1], mx[:, 1:2], ALU.max)
        _finalize_max(k, psb, mx[:, 0:1], D["ar4_in"], D["ar4_out"], [(8, 9)])
    k.barrier()


def ph8(k, psb):
    D, T = k.D, k.T
    with ExitStack() as st:
        t1 = k.sb("t8", (128, 2 * 2048), F32, st)
        u1 = k.sb("u8", (128, 2 * 2048), F32, st)
        tq = k.sb("tq8", (128, 2048), F32, st)
        uq = k.sb("uq8", (128, 2048), F32, st)
        uv = k.sb("uv8", (128, 2048), F32, st)
        mx = k.sb("mx8", (128, 2), F32, st)
        k.mset(mx[:], 0.0)
        for it in range(24):
            b = it % 2
            tsl = t1[:, b * 2048:(b + 1) * 2048]
            usl = u1[:, b * 2048:(b + 1) * 2048]
            k.dma("sync", tsl, D["t_sp"][it * 128:(it + 1) * 128, :],
                  stream="t8%d" % b)
            k.dma("sync", usl, D["up_sp"][it * 128:(it + 1) * 128, :],
                  stream="u8%d" % b)
            _qdq_chain_tile(k, tsl, 8, 9, tq[:])
            _qdq_chain_tile(k, usl, 4, 5, uq[:])
            k.tt(uv[:], tq[:], uq[:], ALU.mult)
            k.dma("scalar", D["u_sp"][it * 128:(it + 1) * 128, :], uv[:],
                  stream="u8s")
            k.red(mx[:, 1:2], uv[:], ALU.max, absv=True)
            k.tt(mx[:, 0:1], mx[:, 0:1], mx[:, 1:2], ALU.max)
        _finalize_max(k, psb, mx[:, 0:1], D["ar5_in"], D["ar5_out"], [(10, 11)])
    k.mset(k.T["scal"][:, 12:16], 0.0)
    k.mset(k.T["scal"][:, 0:1], 0.0)
    k.dbg("scal_all", k.T["scal"][:, 0:16], None)
    k.dbg("t_sp0", D["t_sp"][0:128, :], None)
    k.dbg("u_sp0", D["u_sp"][0:128, :], None)
    k.barrier()


def ph9(k, psb):
    D, T = k.D, k.T
    k.dbg("s_sp_all", D["s_sp"][:], None)
    with ExitStack() as st:
        _dequant_weight(k, psb, st, D["Wd"], None, DM, IFF, "d",
                        dst_dram=D["WdTH"])
    with ExitStack() as st:
        u9 = k.sb("u9", (128, 2 * 512), F32, st)
        i9 = k.sb("i9", (128, 512), I32, st)
        ob9 = k.sb("ob9", (128, 24 * 512), BF16, st)
        wd9 = k.sb("wd9", (128, 24 * DM), BF16, st)
        gfn = k.sb("gfn9", (128, 512), F32, st)
        sg9 = k.sb("sg9", (128, 8 * 512), F32, st)
        s9 = k.sb("s9", (128, 8 * 512), F32, st)
        xre = k.sb("xre9", (128, 4 * DM), F32, st)
        ost = k.sb("ost9", (128, DM), F32, st)
        for tt in range(4):
            k.dma("sync", xre[:].rearrange("p (a c) -> p a c", a=4),
                  D["x"].rearrange("(a p) c -> p a c", p=128)
                  [:, tt * 4:(tt + 1) * 4, :], stream="x9l")
            k.dma("sync", s9[:].rearrange("p (a c) -> p a c", a=8),
                  D["s_sp"].rearrange("(a p) t -> p a t", p=128)
                  [:, :, tt * SEG:(tt + 1) * SEG], stream="s9l")
            for kc in range(24):
                b = kc % 2
                usl = u9[:, b * 512:(b + 1) * 512]
                k.dma("sync", usl,
                      D["u_sp"][kc * 128:(kc + 1) * 128, tt * SEG:(tt + 1) * SEG],
                      stream="u9%d" % b)
                k.ts(i9[:], usl, T["scal"][:, 10:11], None, ALU.mult)
                k.ts(ob9[:, kc * 512:(kc + 1) * 512], i9[:],
                     T["scal"][:, 11:12], None, ALU.mult)
            for oc in range(8):
                bk = oc % 4
                for kc in range(24):
                    if oc == 0 and tt == 0:
                        k.dma("sync", wd9[:, kc * DM:(kc + 1) * DM],
                              D["WdTH"].rearrange("(a p) c -> p a c", p=128)
                              [:, kc, :], stream="wd9%d" % (kc % 2))
                    k.mm(psb[bk][0:128, 0:512],
                         wd9[:, kc * DM + oc * 128: kc * DM + (oc + 1) * 128],
                         ob9[:, kc * 512:(kc + 1) * 512],
                         start=(kc == 0), stop=(kc == 23))
                k.act(gfn[:], psb[bk][0:128, 0:512], AF.Silu)
                k.tt(sg9[:, oc * 512:(oc + 1) * 512],
                     s9[:, oc * 512:(oc + 1) * 512], gfn[:], ALU.mult)
            for tb in range(4):
                for oc in range(8):
                    bkp = 4 + (oc % 2)
                    k.tr(psb[bkp][0:128, 0:128],
                         sg9[:, oc * 512 + tb * 128: oc * 512 + (tb + 1) * 128])
                    k.ex("vector", lambda e, tb=tb, oc=oc, bkp=bkp:
                         e.scalar_tensor_tensor(
                             ost[:, oc * 128:(oc + 1) * 128],
                             xre[:, tb * DM + oc * 128: tb * DM + (oc + 1) * 128],
                             0.0, psb[bkp][0:128, 0:128], ALU.add, ALU.add),
                         rd=[xre[:, 0:1], psb[bkp][0:1, 0:1]],
                         wr=[ost[:, 0:1]])
                k.dma("scalar",
                      D["out"][tt * SEG + tb * 128: tt * SEG + (tb + 1) * 128, :],
                      ost[:], stream="outw")
    k.barrier()


import ml_dtypes

def host_shards(inp):
    f32 = np.float32
    x_full = np.ascontiguousarray(inp["L0_attn_u"], dtype=f32)
    Wp = inp["L0_attn_in_proj_w"].astype(f32)
    WpT = np.ascontiguousarray(Wp.T)
    WoT = np.ascontiguousarray(inp["L0_attn_out_proj_w"].astype(f32).T)
    Wg = np.ascontiguousarray(inp["L0_ffn_w_gate"], dtype=f32)
    Wu = np.ascontiguousarray(inp["L0_ffn_w_up"], dtype=f32)
    Wd = np.ascontiguousarray(inp["L0_ffn_w_down"], dtype=f32)

    smalls = np.zeros((8, 128), f32)
    smalls[0, 0:32] = inp["L0_attn_D"]
    smalls[1, 0:32] = inp["L0_attn_dt_bias"]
    smalls[2, 0:32] = inp["L0_attn_A_log"]
    smalls[3, 0:64] = inp["L0_attn_B_norm_w"]
    smalls[4, 0:64] = inp["L0_attn_C_norm_w"]

    bias_bc = np.zeros((128, 2 * H * N), f32)
    bias_bc[:, 0:H * N] = inp["L0_attn_B_bias"].reshape(1, -1)
    bias_bc[:, H * N:] = inp["L0_attn_C_bias"].reshape(1, -1)
    bias_bc = bias_bc.astype(ml_dtypes.bfloat16)
    wn_bc = np.broadcast_to(inp["L0_attn_norm_w"].astype(f32)[None, :],
                            (128, DI)).astype(ml_dtypes.bfloat16).copy()
    wbc = np.zeros((128, 2 * N), f32)
    wbc[:, 0:N] = inp["L0_attn_B_norm_w"]
    wbc[:, N:] = inp["L0_attn_C_norm_w"]

    idn_f = np.eye(128, dtype=f32)
    idn_b = np.eye(128, dtype=f32).astype(ml_dtypes.bfloat16)
    s_ = np.arange(128)[:, None]; i_ = np.arange(128)[None, :]
    tribig = np.where(i_ >= s_, BIG, 0.0).astype(f32).astype(ml_dtypes.bfloat16)

    maps = []
    for c in range(8):
        seq, lane = c // 4, c % 4
        rows = []
        for w in range(NW):
            g = 4 * w + lane
            rows.append(x_full[seq, g * SEG:(g + 1) * SEG, :])
        x_core = np.concatenate(rows, axis=0)
        xnext = np.zeros((NW, DM), f32)
        for w in range(NW):
            g = 4 * w + lane
            if g + 1 < 16:
                xnext[w] = x_full[seq, (g + 1) * SEG, :]
        masks = np.zeros((128, 16), f32)
        for j in range(4):
            masks[:, j] = 1.0 if j < lane else 0.0
            masks[:, 8 + j] = 1.0 if j == lane else 0.0
        for w in range(NW):
            g = 4 * w + lane
            masks[:, 4 + w] = 1.0 if (g + 1 < 16) else 0.0
        rows3 = np.zeros((1, 384), f32)
        rows3[0, 0:128] = 1.0
        rows3[0, 128:192] = 1.0
        rows3[0, 320:384] = 1.0
        e0mat = np.zeros((128, 128), f32); e0mat[0, :] = 1.0
        kk = np.arange(128)
        mska = (kk[:, None] < kk[None, :]).astype(f32).astype(ml_dtypes.bfloat16)
        mskb = (kk[:, None] >= kk[None, :]).astype(f32) * (-1.0e30)
        mskb4 = np.tile(mskb, (1, 4)).astype(ml_dtypes.bfloat16)
        maps.append(dict(
            x=np.ascontiguousarray(x_core), xnext=xnext, rows3=rows3,
            e0mat=e0mat, mska=mska, mskb4=mskb4,
            WpT=WpT, Wpbf=WpT.astype(ml_dtypes.bfloat16),
            WoT=WoT, Wg=Wg, Wu=Wu, Wd=Wd,
            smalls=smalls, bias_bc=bias_bc, wn_bc=wn_bc, wbc_bc=wbc,
            idn_f=idn_f, idn_b=idn_b, tribig=tribig, masks=masks,
        ))
    return maps


def kernel(**inputs):
    from concourse.bass_utils import run_bass_kernel_spmd
    nc, _ = build(9, ())
    maps = host_shards(inputs)
    r = run_bass_kernel_spmd(nc, maps, core_ids=list(range(8)))
    res = r.results
    out = np.zeros((2, 8192, 1024), np.float32)
    for c in range(8):
        seq, lane = c // 4, c % 4
        o = np.asarray(res[c]["out"], dtype=np.float32)
        for w in range(NW):
            g = 4 * w + lane
            out[seq, g * SEG:(g + 1) * SEG, :] = o[w * SEG:(w + 1) * SEG, :]
    return out
